# version 09
# baseline (speedup 1.0000x reference)
import numpy as np
from contextlib import ExitStack
import concourse.bass as bass
import concourse.mybir as mybir
from concourse.bass_utils import run_bass_kernel_spmd

F32 = mybir.dt.float32
BF16 = mybir.dt.bfloat16
I32 = mybir.dt.int32
AF = mybir.ActivationFunctionType
ALU = mybir.AluOpType
AX = mybir.AxisListType

B = 2
S = 4096
D = 2048
NH = 16
DH = 128
HF = 8192
TOK = 1024
HALF = 2048
HPM = 8
INV_SQRT_DH = float(np.float32(1.0) / np.sqrt(np.float32(DH)))
RMS_EPS = float(np.finfo(np.float32).eps)
LN_EPS = 1e-5
MAGIC = 19_750_623

DT = D // 128
NQT = S // 128
GRP = 4
NGRP = NQT // GRP
NCH = HALF // 512

STAGE = 99


class Sem:

    def __init__(self, handle):
        self.h = handle
        self.val = 0

    def inc(self, ins, k=1):
        ins.then_inc(self.h, k)
        self.val += k
        return self.val


def _flag_row(sync_id, member):
    return 2 * sync_id + member


def build_program(detect_races=True):
    nc = bass.Bass("TRN2", target_bir_lowering=False, num_devices=8,
                   detect_race_conditions=detect_races,
                   dynamic_dma_scratch_size=16384)

    x_half = nc.dram_tensor("x_half", [HALF, D], F32, kind="ExternalInput").ap()
    x_own = nc.dram_tensor("x_own", [TOK, D], F32, kind="ExternalInput").ap()
    w_qk = nc.dram_tensor("w_qk", [D, 2048], F32, kind="ExternalInput").ap()
    w_v = nc.dram_tensor("w_v", [D, 2048], F32, kind="ExternalInput").ap()
    w_o = nc.dram_tensor("w_o", [D, D], F32, kind="ExternalInput").ap()
    wgu_half = nc.dram_tensor("wgu_half", [HF, D], F32, kind="ExternalInput").ap()
    wp_half = nc.dram_tensor("wp_half", [1024, HF], F32, kind="ExternalInput").ap()
    ids = nc.dram_tensor("ids", [1, 16], I32, kind="ExternalInput").ap()
    out = nc.dram_tensor("out", [TOK, D], F32, kind="ExternalOutput").ap()

    xb16 = nc.dram_tensor("xb16", [DT, HALF, 128], BF16)
    xnT_sh = nc.dram_tensor("xnT_sh", [D, S], BF16, addr_space="Shared")
    wqgu_blk = nc.dram_tensor("wqgu_blk", [DT, 2 * HF, 128], BF16, addr_space="Shared")
    wqp_blk = nc.dram_tensor("wqp_blk", [HF // 128, D, 128], BF16, addr_space="Shared")
    colsum_sh = nc.dram_tensor("colsum_sh", [NH, S], F32, addr_space="Shared")
    flags_sh = nc.dram_tensor("flags_sh", [16, 16], I32, addr_space="Shared")
    hT_loc = nc.dram_tensor("hT_loc", [HF, TOK], BF16)
    if STAGE < 99:
        dbg = nc.dram_tensor("dbg", [TOK, D], F32, kind="ExternalOutput").ap()

    es = ExitStack()
    with es:
        sb = lambda *a, **k: es.enter_context(nc.sbuf_tensor(*a, **k))
        sem = lambda n: es.enter_context(nc.semaphore(n))
        sids = sb("sids", [1, 16], I32)
        ones16 = sb("ones16", [1, 128], BF16)
        onescol = sb("onescol", [128, 1], BF16)
        ones32 = sb("ones32", [1, 128], F32)
        smagic = sb("smagic", [1, 16], I32)
        consts = sb("consts", [128, 4], F32)
        sflag = sb("sflag", [1, 32], I32)
        h_in = sem("s_in"); h_ln = sem("s_ln"); h_lnv = sem("s_lnv")
        h_mm = sem("s_mm"); h_st = sem("s_st"); h_poll = sem("s_poll")
        h_sync = sem("s_sync")
        h_ch = sem("s_ch")
        h_tp = sem("s_tp")
        h_xst = sem("s_xst")
        block = es.enter_context(nc.Block())
        s_in, s_ln, s_lnv, s_mm, s_st = Sem(h_in), Sem(h_ln), Sem(h_lnv), Sem(h_mm), Sem(h_st)
        s_sync = Sem(h_sync)
        s_chn = Sem(h_ch)
        s_tp = Sem(h_tp)
        s_xst = Sem(h_xst)

        esA = ExitStack()
        with esA:
            sbA = lambda *a, **k: esA.enter_context(nc.sbuf_tensor(*a, **k))
            psA = lambda *a, **k: esA.enter_context(nc.psum_tensor(*a, **k))
            xt = sbA("xt", [128, DT, 512], BF16, is_transpose_dest=True)
            xt2 = sbA("xt2", [128, DT, 512], BF16)
            rowm = sbA("rowm", [1, 512], F32)
            rows2 = sbA("rows2", [1, 512], F32)
            rowtmp = sbA("rowtmp", [1, 512], F32)
            rowvar = sbA("rowvar", [1, 512], F32)
            rowrstd = sbA("rowrstd", [1, 512], F32)
            rowm16 = sbA("rowm16", [1, 512], BF16)
            rowrstd16 = sbA("rowrstd16", [1, 512], BF16)
            mbc = sbA("mbc", [128, 512], BF16)
            rbc = sbA("rbc", [128, 512], BF16)
            cen = sbA("cen", [128, DT, 512], BF16)
            xnt_c = sbA("xnt_c", [128, DT, 512], BF16)
            pS = psA("pS", [1, 512], F32)
            pS2 = psA("pS2", [1, 512], F32)
            pM = psA("pM", [128, 512], F32)
            pR = psA("pR", [128, 512], F32)
            @block.gpsimd
            def _(g):
                g.memset(smagic[:, :], MAGIC)
                s_in.inc(g.dma_start(out=sids[:, :], in_=ids), 16)
                xr = x_half.rearrange("t (dt dd) -> dt t dd", dd=128)
                for dt in range(DT):
                    s_in.inc(g.dma_start(out=xb16[dt, :, :], in_=xr[dt, :, :]), 16)

            @block.vector
            def _(v):
                v.memset(ones16[:, :], 1.0)
                v.memset(ones32[:, :], 1.0)
                v.memset(consts[:, 0:1], LN_EPS)
                v.memset(consts[:, 1:2], RMS_EPS)
                s_lnv.inc(v.memset(onescol[:, :], 1.0))

            for c in range(NCH):
                @block.sync
                def _(sy, c=c):
                    if c == 0:
                        sy.wait_ge(h_in, 272)
                    else:
                        sy.wait_ge(h_xst, s_xst.val)
                    for dt in range(DT):
                        ins = sy.dma_start_transpose(
                            xt[:, dt, :], xb16[dt, 512 * c:512 * (c + 1), :])
                        s_tp.inc(ins, 16)

                @block.vector
                def _(v, c=c):
                    v.wait_ge(h_tp, s_tp.val)
                    s_lnv.inc(v.tensor_tensor(xt2[:, :, :], xt[:, :, :], xt[:, :, :], ALU.mult))

                @block.tensor
                def _(t, c=c):
                    t.wait_ge(h_lnv, s_lnv.val)
                    for dt in range(DT):
                        t.matmul(pS[:, :], onescol[:, :], xt[:, dt, :],
                                 start=(dt == 0), stop=(dt == DT - 1))
                    for dt in range(DT):
                        ins = t.matmul(pS2[:, :], onescol[:, :], xt2[:, dt, :],
                                       start=(dt == 0), stop=(dt == DT - 1))
                    s_mm.inc(ins)

                @block.scalar
                def _(s, c=c):
                    s.wait_ge(h_mm, s_mm.val)
                    s.activation(rowm[:, :], pS[:, :], AF.Copy, scale=1.0 / D)
                    s_ln.inc(s.activation(rows2[:, :], pS2[:, :], AF.Copy, scale=1.0 / D))

                @block.vector
                def _(v, c=c):
                    v.wait_ge(h_ln, s_ln.val)
                    s_chn.inc(v.tensor_tensor(rowtmp[:, :], rowm[:, :], rowm[:, :], ALU.mult))
                    v.wait_ge(h_ch, s_chn.val)
                    s_lnv.inc(v.tensor_tensor(rowvar[:, :], rows2[:, :], rowtmp[:, :], ALU.subtract))

                @block.scalar
                def _(s, c=c):
                    s.wait_ge(h_lnv, s_lnv.val)
                    s_chn.inc(s.activation(rowtmp[:, :], rowvar[:, :], AF.Ln, bias=consts[0:1, 0:1]))
                    s.wait_ge(h_ch, s_chn.val)
                    s_ln.inc(s.activation(rowrstd[:, :], rowtmp[:, :], AF.Exp, scale=-0.5))

                @block.vector
                def _(v, c=c):
                    v.wait_ge(h_ln, s_ln.val)
                    v.tensor_copy(rowm16[:, :], rowm[:, :])
                    s_lnv.inc(v.tensor_copy(rowrstd16[:, :], rowrstd[:, :]))

                @block.tensor
                def _(t, c=c):
                    t.wait_ge(h_lnv, s_lnv.val)
                    t.matmul(pM[:, :], ones16[:, :], rowm16[:, :], start=True, stop=True)
                    s_mm.inc(t.matmul(pR[:, :], ones16[:, :], rowrstd16[:, :], start=True, stop=True))

                @block.vector
                def _(v, c=c):
                    v.wait_ge(h_mm, s_mm.val)
                    v.tensor_copy(mbc[:, :], pM[:, :])
                    s_chn.inc(v.tensor_copy(rbc[:, :], pR[:, :]))
                    v.wait_ge(h_ch, s_chn.val)
                    s_chn.inc(v.tensor_tensor(cen[:, :, :], xt[:, :, :], 
                                              mbc[:, :].unsqueeze(1).broadcast_to([128, DT, 512]), ALU.subtract))
                    v.wait_ge(h_ch, s_chn.val)
                    s_lnv.inc(v.tensor_tensor(xnt_c[:, :, :], cen[:, :, :],
                                              rbc[:, :].unsqueeze(1).broadcast_to([128, DT, 512]), ALU.mult))

                @block.gpsimd
                def _(g, c=c):
                    g.wait_ge(h_lnv, s_lnv.val)
                    with g.register(f"mem{c}") as mem:
                        g.reg_load(mem, sids[0:1, 1:2])
                        with g.If_eq(mem, 0):
                            g.dma_start(
                                out=xnT_sh.ap()[:, 512 * c:512 * (c + 1)].rearrange(
                                    "(dt p) t -> p dt t", p=128),
                                in_=xnt_c[:, :, :]).then_inc(h_xst, 16)
                        with g.Else():
                            g.dma_start(
                                out=xnT_sh.ap()[:, HALF + 512 * c:HALF + 512 * (c + 1)].rearrange(
                                    "(dt p) t -> p dt t", p=128),
                                in_=xnt_c[:, :, :]).then_inc(h_xst, 16)
                    s_xst.val += 16

            @block.gpsimd
            def _(g):
                g.wait_ge(h_xst, s_xst.val)
                with g.register("memS0") as mem:
                    g.reg_load(mem, sids[0:1, 1:2])
                    with g.If_eq(mem, 0):
                        g.dma_start(out=flags_sh[0:1, :], in_=smagic[:, :]).then_inc(h_poll, 16)
                    with g.Else():
                        g.dma_start(out=flags_sh[1:2, :], in_=smagic[:, :]).then_inc(h_poll, 16)
                g.wait_ge(h_poll, 16)
                with g.register("ndS0") as nd, g.register("tgtS0") as tgt:
                    g.reg_mov(nd, 1)
                    g.reg_mov(tgt, 16)
                    with g.While(nd):
                        g.reg_alu(tgt, tgt, 16, ALU.add)
                        g.dma_start(out=sflag[:, 0:32],
                                    in_=flags_sh.ap()[0:2, :].rearrange("a b -> (a b)").unsqueeze(0)
                                    ).then_inc(h_poll, 16)
                        g.wait_ge(h_poll, tgt)
                        with g.register("fv0S0") as fv0, g.register("fv1S0") as fv1:
                            g.reg_load(fv0, sflag[0:1, 0:1])
                            g.reg_load(fv1, sflag[0:1, 16:17])
                            g.reg_alu(fv0, fv0, MAGIC, ALU.not_equal)
                            g.reg_alu(fv1, fv1, MAGIC, ALU.not_equal)
                            g.reg_alu(nd, fv0, fv1, ALU.logical_or)
                s_sync.inc(g.nop())

        esB = ExitStack()
        with esB:
            sbB = lambda *a, **k: esB.enter_context(nc.sbuf_tensor(*a, **k))
            psB = lambda *a, **k: esB.enter_context(nc.psum_tensor(*a, **k))
            xnc2 = [sbB(f"xnc{i}", [128, DT, 512], BF16) for i in range(2)]
            xnc = xnc2[0]
            whead = sbB("whead", [128, DT, 256], BF16)
            QT = sbB("QT", [128, S], BF16)
            KT = sbB("KT", [128, S], BF16)
            racc = sbB("racc", [128, 8 * GRP], F32)
            rsum = sbB("rsum", [128, 2 * GRP], F32)
            arec = sbB("arec", [128, 2 * GRP], F32)
            a16 = sbB("a16", [128, 2 * GRP], BF16)
            cstage = sbB("cstage", [1, S], F32)
            if STAGE == 35:
                cget = sbB("cget", [1, 8, 512], F32)
                aget = sbB("aget", [128, 4, GRP], F32)
                eget = sbB("eget", [128, 2048], BF16)
                rget = sbB("rget", [128, GRP], F32)
                r2get = sbB("r2get", [128, GRP * 4], F32)
            pBig = [psB(f"pBig{i}", [128, 1024], F32) for i in range(4)]
            pQK = [pBig[i][:, 0:512] for i in range(4)]
            h_wl = es.enter_context(nc.semaphore("s_wl"))
            h_xc = es.enter_context(nc.semaphore("s_xc"))
            h_xcg = es.enter_context(nc.semaphore("s_xcg"))
            h_qp = es.enter_context(nc.semaphore("s_qp"))
            h_qe = es.enter_context(nc.semaphore("s_qe"))
            h_sc = es.enter_context(nc.semaphore("s_sc"))
            h_ex = es.enter_context(nc.semaphore("s_ex"))
            h_cse = es.enter_context(nc.semaphore("s_cse"))
            h_csb = es.enter_context(nc.semaphore("s_csb"))
            h_csv = es.enter_context(nc.semaphore("s_csv"))
            h_cst = es.enter_context(nc.semaphore("s_cst"))
            s_wl, s_xc, s_qp, s_qe = Sem(h_wl), Sem(h_xc), Sem(h_qp), Sem(h_qe)
            s_xcg = Sem(h_xcg)
            s_sc, s_ex, s_cse = Sem(h_sc), Sem(h_ex), Sem(h_cse)
            s_csb, s_csv, s_cst = Sem(h_csb), Sem(h_csv), Sem(h_cst)

            if STAGE == 35:
                @block.vector
                def _(v):
                    v.memset(cget[:, :, :], 0.0)
                    v.memset(eget[:, :], 0.0)
                    v.memset(rget[:, :], 0.0)
                    v.memset(r2get[:, :], 0.0)
                    v.memset(aget[:, :, :], 0.0)

            def emit_attention_head(hl, Egrp):
                @block.vector
                def _(v):
                    if s_cst.val > 0:
                        v.wait_ge(h_cst, s_cst.val)
                    v.memset(cstage[:, :], 0.0)

                cse0 = s_cse.val

                def emit_scores(g_i):
                    @block.tensor
                    def _(t, g_i=g_i):
                        if g_i == 0:
                            t.wait_ge(h_qe, s_qe.val)
                        for j in range(GRP):
                            i = g_i * GRP + j
                            for qtr in range(4):
                                q_idx = s_sc.val
                                if q_idx >= 2:
                                    t.wait_ge(h_ex, q_idx - 1)
                                par = q_idx % 2
                                for kc in range(2):
                                    ins = t.matmul(
                                        pBig[par][:, 512 * kc:512 * (kc + 1)],
                                        QT[:, 128 * i:128 * (i + 1)],
                                        KT[:, 1024 * qtr + 512 * kc:1024 * qtr + 512 * (kc + 1)],
                                        start=True, stop=True)
                                s_sc.inc(ins)

                    @block.scalar
                    def _(s, g_i=g_i):
                        base0 = s_sc.val - 4 * GRP
                        half = g_i % 2
                        for j in range(GRP):
                            for qtr in range(4):
                                base = base0 + 4 * j + qtr + 1
                                s.wait_ge(h_sc, base)
                                if j == 0 and qtr == 0 and g_i >= 2:
                                    s.wait_ge(h_cse, cse0 + g_i - 1)
                                par = (base - 1) % 2
                                ins = s.activation(
                                    Egrp[:, GRP * half + j, 1024 * qtr:1024 * (qtr + 1)],
                                    pBig[par][:, :], AF.Exp, scale=INV_SQRT_DH,
                                    accum_out=racc[:, 16 * half + 4 * j + qtr:16 * half + 4 * j + qtr + 1])
                                s_ex.inc(ins)

                    @block.vector
                    def _(v, g_i=g_i):
                        half = g_i % 2
                        v.wait_ge(h_ex, s_ex.val)
                        s_chn.inc(v.tensor_reduce(rsum[:, 4 * half:4 * (half + 1)],
                                        racc[:, 16 * half:16 * (half + 1)].rearrange(
                                            "p (g q) -> p g q", q=4),
                                        axis=AX.X, op=ALU.add))
                        v.wait_ge(h_ch, s_chn.val)
                        s_chn.inc(v.reciprocal(arec[:, 4 * half:4 * (half + 1)],
                                               rsum[:, 4 * half:4 * (half + 1)]))
                        v.wait_ge(h_ch, s_chn.val)
                        s_cse.inc(v.tensor_copy(a16[:, 4 * half:4 * (half + 1)],
                                                arec[:, 4 * half:4 * (half + 1)]))

                def emit_bursts(g_i):
                    half = g_i % 2

                    @block.tensor
                    def _(t, g_i=g_i, half=half):
                        t.wait_ge(h_cse, cse0 + g_i + 1)
                        for kc in range(8):
                            if s_csb.val >= 2:
                                t.wait_ge(h_csv, s_csb.val - 1)
                            pcs = pBig[2][0:1, 512 * (kc % 2):512 * (kc % 2 + 1)]
                            for j in range(GRP):
                                ins = t.matmul(pcs, a16[:, 4 * half + j:4 * half + j + 1],
                                               Egrp[:, GRP * half + j, 512 * kc:512 * (kc + 1)],
                                               start=(j == 0), stop=(j == GRP - 1))
                            s_csb.inc(ins)

                    @block.vector
                    def _(v, g_i=g_i):
                        base0 = s_csb.val - 8
                        for kc in range(8):
                            v.wait_ge(h_csb, base0 + kc + 1)
                            pcs = pBig[2][0:1, 512 * (kc % 2):512 * (kc % 2 + 1)]
                            ins = v.scalar_tensor_tensor(
                                cstage[0:1, 512 * kc:512 * (kc + 1)],
                                pcs, 1.0,
                                cstage[0:1, 512 * kc:512 * (kc + 1)],
                                ALU.mult, ALU.add)
                            s_csv.inc(ins)

                emit_scores(0)
                for g_i in range(1, NGRP):
                    emit_scores(g_i)
                    emit_bursts(g_i - 1)
                emit_bursts(NGRP - 1)

                @block.gpsimd
                def _(g, hl=hl):
                    g.wait_ge(h_csv, s_csv.val)
                    with g.register(f"memH{hl}") as mem:
                        g.reg_load(mem, sids[0:1, 1:2])
                        with g.If_eq(mem, 0):
                            g.dma_start(out=colsum_sh[hl:hl + 1, :],
                                        in_=cstage[:, :]).then_inc(h_cst, 16)
                        with g.Else():
                            g.dma_start(out=colsum_sh[HPM + hl:HPM + hl + 1, :],
                                        in_=cstage[:, :]).then_inc(h_cst, 16)
                    s_cst.val += 16

            h_qw = es.enter_context(nc.semaphore("s_qw"))
            h_qs = es.enter_context(nc.semaphore("s_qs"))
            h_qv = es.enter_context(nc.semaphore("s_qv"))
            s_qw, s_qs, s_qv = Sem(h_qw), Sem(h_qs), Sem(h_qv)
            QW = 1024
            h_qg = es.enter_context(nc.semaphore("s_qg"))
            s_qg = Sem(h_qg)
            qD = [{
                "w16": sbB(f"qw16_{i}", [128, QW], BF16),
                "t16": sbB(f"qt16_{i}", [128, QW], BF16),
                "tmp1": sbB(f"qtmp1_{i}", [128, QW], BF16),
                "tmp2": sbB(f"qtmp2_{i}", [128, QW], BF16),
                "gsum": sbB(f"qgsum_{i}", [128, QW // 64], F32),
                "scl": sbB(f"qscl_{i}", [128, QW // 64], BF16),
                "rcp": sbB(f"qrcp_{i}", [128, QW // 64], F32),
            } for i in range(2)]

            def quant_tile_emitter(ti, kind, r, cc):
                G = QW // 64
                B = qD[ti % 2]
                qw16, qt16, qtmp1, qtmp2 = B["w16"], B["t16"], B["tmp1"], B["tmp2"]
                qgsum, qscl, qrcp = B["gsum"], B["scl"], B["rcp"]

                @block.gpsimd
                def _(g):
                    if ti > 1:
                        g.wait_ge(h_qv, ti - 1)
                    if kind == "gu":
                        ins = g.dma_start(out=qw16[:, :],
                                          in_=wgu_half[128 * r:128 * (r + 1),
                                                       QW * cc:QW * (cc + 1)])
                    else:
                        ins = g.dma_start(out=qw16[:, :],
                                          in_=wp_half[128 * r:128 * (r + 1),
                                                      QW * cc:QW * (cc + 1)])
                    s_qw.inc(ins, 16)

                qw_target = s_qw.val

                @block.vector
                def _(v):
                    v.wait_ge(h_qw, qw_target)
                    w3 = qw16[:, :].rearrange("p (g s) -> p g s", s=64)
                    s_chn.inc(v.tensor_reduce(qgsum[:, :], w3, axis=AX.X,
                                              op=ALU.add, apply_absolute_value=True))
                    v.wait_ge(h_ch, s_chn.val)
                    s_chn.inc(v.tensor_scalar(qscl[:, :], qgsum[:, :],
                                              1.0 / 64, 1e-8, ALU.mult, ALU.max))
                    v.wait_ge(h_ch, s_chn.val)
                    s_chn.inc(v.reciprocal(qrcp[:, :], qscl[:, :]))

                @block.gpsimd
                def _(g):
                    g.wait_ge(h_ch, s_chn.val)
                    rb = qrcp[:, :].unsqueeze(2).broadcast_to([128, G, 64])
                    ins = g.tensor_tensor(
                        qt16[:, :].rearrange("p (g s) -> p g s", s=64),
                        qw16[:, :].rearrange("p (g s) -> p g s", s=64), rb, ALU.mult)
                    s_qg.inc(ins)

                @block.vector
                def _(v):
                    v.wait_ge(h_qg, s_qg.val)
                    s_chn.inc(v.tensor_scalar(qtmp1[:, :], qt16[:, :], 0.5, None, ALU.is_gt))
                    s_chn.inc(v.tensor_scalar(qtmp2[:, :], qt16[:, :], -0.5, None, ALU.is_lt))
                    v.wait_ge(h_ch, s_chn.val)
                    s_chn.inc(v.tensor_tensor(qt16[:, :], qtmp1[:, :], qtmp2[:, :], ALU.subtract))

                @block.gpsimd
                def _(g):
                    g.wait_ge(h_ch, s_chn.val)
                    sb_ = qscl[:, :].unsqueeze(2).broadcast_to([128, G, 64])
                    ins = g.tensor_tensor(
                        qtmp1[:, :].rearrange("p (g s) -> p g s", s=64),
                        qt16[:, :].rearrange("p (g s) -> p g s", s=64), sb_, ALU.mult)
                    s_qg.inc(ins)

                @block.vector
                def _(v):
                    v.wait_ge(h_qg, s_qg.val)
                    s_chn.inc(v.tensor_tensor(qtmp2[:, :], qtmp1[:, :], qw16[:, :], ALU.subtract))
                    v.wait_ge(h_ch, s_chn.val)
                    if ti > 1:
                        v.wait_ge(h_qs, 16 * (ti - 1))
                    s_chn.inc(v.tensor_tensor(qtmp1[:, :], qw16[:, :], qtmp2[:, :], ALU.add))
                    v.wait_ge(h_ch, s_chn.val)
                    s_qv.inc(v.nop())

                @block.gpsimd
                def _(g):
                    g.wait_ge(h_qv, ti + 1)
                    with g.register(f"memQT{ti}") as mem:
                        g.reg_load(mem, sids[0:1, 1:2])
                        for mm_ in range(2):
                            nsub = QW // 128
                            if kind == "gu":
                                dst0 = HF * mm_ + 128 * r
                                blk = wqgu_blk
                            else:
                                dst0 = 1024 * mm_ + 128 * r
                                blk = wqp_blk
                            with (g.If_eq(mem, 0) if mm_ == 0 else g.Else()):
                                g.dma_start(
                                    out=blk.ap()[nsub * cc:nsub * (cc + 1), dst0:dst0 + 128, :].rearrange(
                                        "dt p dd -> p dt dd"),
                                    in_=qtmp1[:, :].rearrange("p (dt dd) -> p dt dd", dd=128)
                                ).then_inc(h_qs, 16)
                    s_qs.val += 16

            _qtiles = ([("gu", r, c) for r in range(64) for c in range(2048 // QW)]
                       + [("p", r, c) for r in range(8) for c in range(HF // QW)])
            _qstate = {"i": 0}

            def emit_quant(k):
                while k > 0 and _qstate["i"] < len(_qtiles):
                    kind, r, c = _qtiles[_qstate["i"]]
                    quant_tile_emitter(_qstate["i"], kind, r, c)
                    _qstate["i"] += 1
                    k -= 1

            def flush_quant():
                emit_quant(10 ** 9)

            esC = ExitStack()
            with esC:
                Egrp = esC.enter_context(nc.sbuf_tensor("Egrp", [128, 2 * GRP, S], BF16))
                NB = 1 if STAGE in (2, 35) else HPM
                for hl in range(NB):
                    @block.gpsimd
                    def _(g, hl=hl):
                        if hl > 0:
                            g.wait_ge(h_qe, s_qe.val)
                        for qk in range(2):
                            cs = 1024 * qk + 128 * hl
                            ins = g.dma_start(
                                out=whead[:, :, 128 * qk:128 * (qk + 1)],
                                in_=w_qk[:, cs:cs + 128].rearrange(
                                    "(dt p) c -> p dt c", p=128))
                            s_wl.inc(ins, 16)

                    qp_base = s_qp.val
                    qe_base = s_qe.val

                    @block.sync
                    def _(sy, hl=hl, qp_base=qp_base):
                        sy.wait_ge(h_sync, s_sync.val)
                        for c in range(S // 512):
                            if hl > 0 or c >= 2:
                                sy.wait_ge(h_qp, qp_base + max(0, c - 1))
                            s_xc.inc(sy.dma_start(
                                out=xnc2[c % 2][:, :, :],
                                in_=xnT_sh.ap()[:, 512 * c:512 * (c + 1)].rearrange(
                                    "(dt p) t -> p dt t", p=128)), 16)
                    sxc_base = s_xc.val - 16 * 8

                    @block.tensor
                    def _(t, hl=hl, sxc_base=sxc_base, qe_base=qe_base):
                        t.wait_ge(h_wl, s_wl.val)
                        if s_csv.val > 0:
                            t.wait_ge(h_csv, s_csv.val)
                        for c in range(S // 512):
                            t.wait_ge(h_xc, sxc_base + 16 * (c + 1))
                            if c >= 2:
                                t.wait_ge(h_qe, qe_base + c - 1)
                            for qk in range(2):
                                for dt in range(DT):
                                    ins = t.matmul(
                                        pQK[2 * (c % 2) + qk],
                                        whead[:, dt, 128 * qk:128 * (qk + 1)],
                                        xnc2[c % 2][:, dt, :],
                                        start=(dt == 0), stop=(dt == DT - 1))
                                if qk == 1:
                                    s_qp.inc(ins)

                    @block.vector
                    def _(v, hl=hl):
                        base0 = s_qp.val - 8
                        for c in range(S // 512):
                            v.wait_ge(h_qp, base0 + c + 1)
                            v.tensor_copy(QT[:, 512 * c:512 * (c + 1)], pQK[2 * (c % 2)])
                            ins = v.tensor_copy(KT[:, 512 * c:512 * (c + 1)], pQK[2 * (c % 2) + 1])
                            s_qe.inc(ins)

                    if STAGE >= 3:
                        emit_attention_head(hl, Egrp)

                if STAGE >= 4:
                    flush_quant()

        VT = es.enter_context(nc.sbuf_tensor("VT", [128, NH, TOK], BF16))
        esV = ExitStack()
        with esV:
            wv16 = esV.enter_context(nc.sbuf_tensor("wv16", [128, DT, 2048], BF16))
            xncV = esV.enter_context(nc.sbuf_tensor("xncV", [128, DT, 512], BF16))
            pVP = [esV.enter_context(nc.psum_tensor(f"pVP{i}", [128, 512], F32)) for i in range(4)]

            @block.gpsimd
            def _(g):
                g.wait_ge(h_qe, s_qe.val)
                if STAGE >= 3:
                    g.wait_ge(h_cst, s_cst.val)
                s_wl.inc(g.dma_start(
                    out=wv16[:, :, :],
                    in_=w_v.rearrange("(dt p) c -> p dt c", p=128)), 16)

            for oc in range(TOK // 512):
                @block.gpsimd
                def _(g, oc=oc):
                    g.wait_ge(h_wl, s_wl.val)
                    if oc > 0:
                        g.wait_ge(h_qe, s_qe.val)
                    with g.register(f"quadV{oc}") as qd:
                        g.reg_load(qd, sids[0:1, 2:3])
                        for q in range(4):
                            with g.If_eq(qd, q):
                                cs = 1024 * q + 512 * oc
                                g.dma_start(
                                    out=xncV[:, :, :],
                                    in_=xnT_sh.ap()[:, cs:cs + 512].rearrange(
                                        "(dt p) t -> p dt t", p=128)).then_inc(h_xcg, 16)
                    s_xcg.val += 16

                for grp in range(4):
                    @block.tensor
                    def _(t, oc=oc, grp=grp):
                        t.wait_ge(h_xcg, s_xcg.val)
                        if oc == 0 and grp == 0 and s_csv.val > 0:
                            t.wait_ge(h_csv, s_csv.val)
                        if oc > 0 or grp > 0:
                            t.wait_ge(h_qe, s_qe.val)
                        for h2 in range(4):
                            hh = 4 * grp + h2
                            for dt in range(DT):
                                ins = t.matmul(pVP[h2][:, :],
                                               wv16[:, dt, 128 * hh:128 * (hh + 1)],
                                               xncV[:, dt, :],
                                               start=(dt == 0), stop=(dt == DT - 1))
                        s_qp.inc(ins)

                    @block.vector
                    def _(v, oc=oc, grp=grp):
                        v.wait_ge(h_qp, s_qp.val)
                        for h2 in range(4):
                            hh = 4 * grp + h2
                            ins = v.tensor_copy(VT[:, hh, 512 * oc:512 * (oc + 1)],
                                                pVP[h2][:, :])
                        s_qe.inc(ins)

        if STAGE >= 3:
            h_poll1 = es.enter_context(nc.semaphore("s_poll1"))

            @block.gpsimd
            def _(g):
                g.wait_ge(h_cst, s_cst.val)
                with g.register("memC") as mem:
                    g.reg_load(mem, sids[0:1, 1:2])
                    with g.If_eq(mem, 0):
                        g.dma_start(out=flags_sh[2:3, :], in_=smagic[:, :]).then_inc(h_poll1, 16)
                    with g.Else():
                        g.dma_start(out=flags_sh[3:4, :], in_=smagic[:, :]).then_inc(h_poll1, 16)
                g.wait_ge(h_poll1, 16)
                with g.register("ndC") as nd, g.register("tgtC") as tgt:
                    g.reg_mov(nd, 1)
                    g.reg_mov(tgt, 16)
                    with g.While(nd):
                        g.reg_alu(tgt, tgt, 16, ALU.add)
                        g.dma_start(out=sflag[:, 0:32],
                                    in_=flags_sh.ap()[2:4, :].rearrange("a b -> (a b)").unsqueeze(0)
                                    ).then_inc(h_poll1, 16)
                        g.wait_ge(h_poll1, tgt)
                        with g.register("fv0C") as fv0, g.register("fv1C") as fv1:
                            g.reg_load(fv0, sflag[0:1, 0:1])
                            g.reg_load(fv1, sflag[0:1, 16:17])
                            g.reg_alu(fv0, fv0, MAGIC, ALU.not_equal)
                            g.reg_alu(fv1, fv1, MAGIC, ALU.not_equal)
                            g.reg_alu(nd, fv0, fv1, ALU.logical_or)
                s_sync.inc(g.nop())

        if STAGE >= 4:
            h_poll2 = es.enter_context(nc.semaphore("s_poll2"))

            @block.gpsimd
            def _(g):
                g.wait_ge(h_qs, s_qs.val)
                with g.register("memW") as mem:
                    g.reg_load(mem, sids[0:1, 1:2])
                    with g.If_eq(mem, 0):
                        g.dma_start(out=flags_sh[4:5, :], in_=smagic[:, :]).then_inc(h_poll2, 16)
                    with g.Else():
                        g.dma_start(out=flags_sh[5:6, :], in_=smagic[:, :]).then_inc(h_poll2, 16)
                g.wait_ge(h_poll2, 16)
                with g.register("ndW") as nd, g.register("tgtW") as tgt:
                    g.reg_mov(nd, 1)
                    g.reg_mov(tgt, 16)
                    with g.While(nd):
                        g.reg_alu(tgt, tgt, 16, ALU.add)
                        g.dma_start(out=sflag[:, 0:32],
                                    in_=flags_sh.ap()[4:6, :].rearrange("a b -> (a b)").unsqueeze(0)
                                    ).then_inc(h_poll2, 16)
                        g.wait_ge(h_poll2, tgt)
                        with g.register("fv0W") as fv0, g.register("fv1W") as fv1:
                            g.reg_load(fv0, sflag[0:1, 0:1])
                            g.reg_load(fv1, sflag[0:1, 16:17])
                            g.reg_alu(fv0, fv0, MAGIC, ALU.not_equal)
                            g.reg_alu(fv1, fv1, MAGIC, ALU.not_equal)
                            g.reg_alu(nd, fv0, fv1, ALU.logical_or)
                s_sync.inc(g.nop())

        if STAGE >= 5:
            g_sb = es.enter_context(nc.sbuf_tensor("g_sb", [128, TOK // 128, D], BF16))
            h_fw = es.enter_context(nc.semaphore("s_fw"))
            h_fx = es.enter_context(nc.semaphore("s_fx"))
            h_fp = es.enter_context(nc.semaphore("s_fp"))
            h_fe = es.enter_context(nc.semaphore("s_fe"))
            h_fh = es.enter_context(nc.semaphore("s_fh"))
            h_fs = es.enter_context(nc.semaphore("s_fs"))
            s_fw, s_fx, s_fp, s_fe = Sem(h_fw), Sem(h_fx), Sem(h_fp), Sem(h_fe)
            s_fh, s_fs = Sem(h_fh), Sem(h_fs)

            esF = ExitStack()
            with esF:
                sbF = lambda *a, **k: esF.enter_context(nc.sbuf_tensor(*a, **k))
                psF = lambda *a, **k: esF.enter_context(nc.psum_tensor(*a, **k))
                inv_col = sbF("inv_col", [128, 8], F32)
                pcolF = psF("pcolF", [128, 8], F32)

                h_fpg = es.enter_context(nc.semaphore("s_fpg"))
                h_fpu = es.enter_context(nc.semaphore("s_fpu"))
                h_feu = es.enter_context(nc.semaphore("s_feu"))
                h_f2 = es.enter_context(nc.semaphore("s_f2"))
                h_fpz = es.enter_context(nc.semaphore("s_fpz"))
                s_fpg, s_fpu, s_feu, s_f2 = Sem(h_fpg), Sem(h_fpu), Sem(h_feu), Sem(h_f2)
                s_fpz = Sem(h_fpz)

                esG = ExitStack()
                with esG:
                    xno = esG.enter_context(nc.sbuf_tensor("xno", [128, DT, TOK], BF16))
                    invrow = esG.enter_context(nc.sbuf_tensor("invrow", [1, TOK], F32))
                    hsrow = esG.enter_context(nc.sbuf_tensor("hsrow", [1, TOK], F32))

                    @block.gpsimd
                    def _(g):
                        g.wait_ge(h_sync, s_sync.val)
                        g.wait_ge(h_qe, s_qe.val)
                        with g.register("quadF") as qd:
                            g.reg_load(qd, sids[0:1, 2:3])
                            for q in range(4):
                                with g.If_eq(qd, q):
                                    g.dma_start(
                                        out=xno[:, :, :],
                                        in_=xnT_sh.ap()[:, 1024 * q:1024 * (q + 1)].rearrange(
                                            "(dt p) t -> p dt t", p=128)).then_inc(h_fx, 16)
                        s_fx.val += 16

                    gat = [esG.enter_context(nc.sbuf_tensor(f"gat{i}", [128, DT, 512], BF16)) for i in range(2)]
                    upt = [esG.enter_context(nc.sbuf_tensor(f"upt{i}", [128, DT, 512], BF16)) for i in range(2)]
                    sgb = esG.enter_context(nc.sbuf_tensor("sgb", [128, 1024], BF16))
                    upe = esG.enter_context(nc.sbuf_tensor("upe", [128, 1024], BF16))
                    hTt = [esG.enter_context(nc.sbuf_tensor(f"hTt{i}", [128, 1024], BF16)) for i in range(2)]
                    h2t = esG.enter_context(nc.sbuf_tensor("h2t", [128, 1024], BF16))
                    h2row = esG.enter_context(nc.sbuf_tensor("h2row", [1, TOK], F32))
                    pg = esG.enter_context(nc.psum_tensor("pg", [128, 1024], F32))
                    pu = esG.enter_context(nc.psum_tensor("pu", [128, 1024], F32))
                    ph2 = esG.enter_context(nc.psum_tensor("ph2", [1, 1024], F32))

                    @block.vector
                    def _(v):
                        v.memset(h2row[:, :], 0.0)

                    NGF = 16
                    fpg0, fpu0, fs0, fe0, feu0, fh0, f20 = (s_fpg.val, s_fpu.val,
                        s_fs.val, s_fe.val, s_feu.val, s_fh.val, s_f2.val)

                    for hg in range(NGF):
                        @block.sync
                        def _(sy, hg=hg):
                            if hg == 0:
                                sy.wait_ge(h_sync, s_sync.val)
                                sy.wait_ge(h_qe, s_qe.val)
                                sy.wait_ge(h_cst, s_cst.val)
                            if hg >= 2:
                                sy.wait_ge(h_fpu, fpu0 + 4 * (hg - 1))
                            for dt in range(DT):
                                sy.dma_start_transpose(
                                    gat[hg % 2][:, dt, :],
                                    wqgu_blk[dt, 512 * hg:512 * (hg + 1), :]
                                ).then_inc(h_fw, 16)
                                sy.dma_start_transpose(
                                    upt[hg % 2][:, dt, :],
                                    wqgu_blk[dt, HF + 512 * hg:HF + 512 * (hg + 1), :]
                                ).then_inc(h_fw, 16)
                            s_fw.val += 32 * DT

                        def emit_h2_mm(t, ht):
                            htl = ht % 4
                            hgl = ht // 4
                            t.wait_ge(h_fe, fe0 + ht + 1)
                            if htl == 0 and hgl >= 1:
                                t.wait_ge(h_fpz, hgl)
                            for ch in range(2):
                                ins = t.matmul(ph2[0:1, 512 * ch:512 * (ch + 1)],
                                               onescol[:, :],
                                               h2t[:, 512 * ch:512 * (ch + 1)],
                                               start=(htl == 0), stop=(htl == 3))
                            s_f2.inc(ins)

                        @block.tensor
                        def _(t, hg=hg):
                            t.wait_ge(h_fw, s_fw.val)
                            if hg == 0:
                                t.wait_ge(h_fx, s_fx.val)
                            for htl in range(4):
                                ht = 4 * hg + htl
                                if ht > 0:
                                    t.wait_ge(h_fs, fs0 + ht)
                                for ch in range(2):
                                    for dt in range(DT):
                                        ins = t.matmul(pg[:, 512 * ch:512 * (ch + 1)],
                                                 gat[hg % 2][:, dt, 128 * htl:128 * (htl + 1)],
                                                 xno[:, dt, 512 * ch:512 * (ch + 1)],
                                                 start=(dt == 0), stop=(dt == DT - 1))
                                s_fpg.inc(ins)
                                if ht > 0:
                                    t.wait_ge(h_feu, feu0 + ht)
                                for ch in range(2):
                                    for dt in range(DT):
                                        ins = t.matmul(pu[:, 512 * ch:512 * (ch + 1)],
                                                       upt[hg % 2][:, dt, 128 * htl:128 * (htl + 1)],
                                                       xno[:, dt, 512 * ch:512 * (ch + 1)],
                                                       start=(dt == 0), stop=(dt == DT - 1))
                                s_fpu.inc(ins)
                                if ht >= 1:
                                    emit_h2_mm(t, ht - 1)
                                if ht == 63:
                                    emit_h2_mm(t, 63)

                        @block.scalar
                        def _(s, hg=hg):
                            for htl in range(4):
                                ht = 4 * hg + htl
                                s.wait_ge(h_fpg, fpg0 + ht + 1)
                                if ht >= 1:
                                    s.wait_ge(h_fe, fe0 + ht)
                                ins = s.activation(sgb[:, :], pg[:, :], AF.Silu)
                                s_fs.inc(ins)

                        @block.vector
                        def _(v, hg=hg):
                            for htl in range(4):
                                ht = 4 * hg + htl
                                v.wait_ge(h_fpu, fpu0 + ht + 1)
                                if ht >= 1:
                                    v.wait_ge(h_fe, fe0 + ht)
                                s_feu.inc(v.tensor_copy(upe[:, :], pu[:, :]))
                                v.wait_ge(h_fs, fs0 + ht + 1)
                                v.wait_ge(h_feu, s_feu.val)
                                if ht >= 2:
                                    v.wait_ge(h_fh, fh0 + 16 * (ht - 1))
                                s_chn.inc(v.tensor_tensor(hTt[ht % 2][:, :], sgb[:, :],
                                                          upe[:, :], ALU.mult))
                                v.wait_ge(h_ch, s_chn.val)
                                if ht >= 1:
                                    v.wait_ge(h_f2, f20 + ht)
                                s_chn.inc(v.tensor_tensor(h2t[:, :], hTt[ht % 2][:, :],
                                                          hTt[ht % 2][:, :], ALU.mult))
                                v.wait_ge(h_ch, s_chn.val)
                                s_fe.inc(v.nop())

                        @block.vector
                        def _(v, hg=hg):
                            v.wait_ge(h_f2, f20 + 4 * (hg + 1))
                            for ch in range(2):
                                ins = v.scalar_tensor_tensor(
                                    h2row[0:1, 512 * ch:512 * (ch + 1)],
                                    ph2[0:1, 512 * ch:512 * (ch + 1)], 1.0,
                                    h2row[0:1, 512 * ch:512 * (ch + 1)],
                                    ALU.mult, ALU.add)
                            s_chn.inc(ins)
                            v.wait_ge(h_ch, s_chn.val)
                            s_fpz.inc(v.nop())

                        @block.gpsimd
                        def _(g, hg=hg):
                            for htl in range(4):
                                ht = 4 * hg + htl
                                g.wait_ge(h_fe, fe0 + ht + 1)
                                g.dma_start(out=hT_loc[128 * ht:128 * (ht + 1), :],
                                            in_=hTt[ht % 2][:, :]).then_inc(h_fh, 16)
                                s_fh.val += 16


                    @block.vector
                    def _(v):
                        v.wait_ge(h_f2, s_f2.val)
                        s_chn.inc(v.tensor_copy(hsrow[:, :], h2row[:, :]))

                    @block.scalar
                    def _(s):
                        s.wait_ge(h_ch, s_chn.val)
                        s_chn.inc(s.activation(invrow[:, :], hsrow[:, :], AF.Ln,
                                               scale=1.0 / HF, bias=consts[0:1, 1:2]))
                        s.wait_ge(h_ch, s_chn.val)
                        s_chn.inc(s.activation(invrow[:, :], invrow[:, :], AF.Exp, scale=-0.5))

                    @block.tensor
                    def _(t):
                        t.wait_ge(h_ch, s_chn.val)
                        for tt in range(8):
                            ins = t.matmul(pcolF[:, tt:tt + 1],
                                           invrow[0:1, 128 * tt:128 * (tt + 1)],
                                           ones32[0:1, 0:1], start=True, stop=True)
                        s_fp.inc(ins)

                    @block.vector
                    def _(v):
                        v.wait_ge(h_fp, s_fp.val)
                        s_fe.inc(v.tensor_copy(inv_col[:, :], pcolF[:, :]))

                esP = ExitStack()
                with esP:
                    wpqH = [esP.enter_context(nc.sbuf_tensor(f"wpqH{i}", [128, 32, 512], BF16))
                            for i in range(3)]
                    hTl = [esP.enter_context(nc.sbuf_tensor(f"hTl{i}", [128, 64, 128], BF16))
                           for i in range(2)]
                    pp = esP.enter_context(nc.psum_tensor("pp", [128, 1024], F32))
                    fpP0 = s_fp.val
                    fhP0 = s_fh.val
                    fwP0 = s_fw.val

                    for k in range(8):
                        dq, hh = k // 2, k % 2

                        @block.sync
                        def _(sy, k=k, dq=dq, hh=hh):
                            if k == 0:
                                sy.wait_ge(h_fe, s_fe.val)
                            if k >= 3:
                                sy.wait_ge(h_fp, fpP0 + 8 * ((k - 3) // 2 + 1))
                            for htl in range(32):
                                ht = 32 * hh + htl
                                sy.dma_start_transpose(
                                    wpqH[k % 3][:, htl, :],
                                    wqp_blk[ht, 512 * dq:512 * (dq + 1), :]
                                ).then_inc(h_fw, 16)
                            s_fw.val += 32 * 16

                    for dq in range(4):
                        for tt in range(8):
                            idx = 8 * dq + tt

                            @block.gpsimd
                            def _(g, dq=dq, tt=tt, idx=idx):
                                if idx == 0:
                                    g.wait_ge(h_fe, s_fe.val)
                                if idx >= 2:
                                    g.wait_ge(h_fp, fpP0 + idx - 1)
                                g.dma_start(
                                    out=hTl[idx % 2][:, :, :],
                                    in_=hT_loc.ap()[:, 128 * tt:128 * (tt + 1)].rearrange(
                                        "(ht p) t -> p ht t", p=128)).then_inc(h_fh, 16)
                                s_fh.val += 16

                            @block.tensor
                            def _(t, dq=dq, tt=tt, idx=idx):
                                t.wait_ge(h_fh, fhP0 + 16 * (idx + 1))
                                t.wait_ge(h_fw, fwP0 + 512 * (2 * dq + 2))
                                par = idx % 2
                                if idx >= 2:
                                    t.wait_ge(h_fs, s_fs.val - 1)
                                for ht in range(64):
                                    ins = t.matmul(pp[:, 512 * par:512 * (par + 1)],
                                                   hTl[idx % 2][:, ht, :],
                                                   wpqH[(2 * dq + ht // 32) % 3][:, ht % 32, :],
                                                   start=(ht == 0), stop=(ht == 63))
                                s_fp.inc(ins)

                        @block.scalar
                        def _(s, dq=dq):
                            base0 = s_fp.val - 8
                            for tt in range(8):
                                s.wait_ge(h_fp, base0 + tt + 1)
                                par = (8 * dq + tt) % 2
                                ins = s.activation(g_sb[:, tt, 512 * dq:512 * (dq + 1)],
                                                   pp[:, 512 * par:512 * (par + 1)],
                                                   AF.Silu, scale=inv_col[:, tt:tt + 1])
                                s_fs.inc(ins)

        if STAGE >= 6:
            h_ow = es.enter_context(nc.semaphore("s_ow"))
            h_op = es.enter_context(nc.semaphore("s_op"))
            h_oe = es.enter_context(nc.semaphore("s_oe"))
            h_oc = es.enter_context(nc.semaphore("s_oc"))
            h_ox = es.enter_context(nc.semaphore("s_ox"))
            s_ow, s_op, s_oe = Sem(h_ow), Sem(h_op), Sem(h_oe)
            s_oc, s_ox = Sem(h_oc), Sem(h_ox)

            esO = ExitStack()
            with esO:
                sbO = lambda *a, **k: esO.enter_context(nc.sbuf_tensor(*a, **k))
                wo16 = sbO("wo16", [128, DT, D], BF16)
                csum16 = sbO("csum16", [1, NH, TOK], BF16)
                xow = sbO("xow", [128, D], F32)
                Pb = sbO("Pb", [128, 512], F32)
                obuf = sbO("obuf", [128, D], F32)
                pbc = esO.enter_context(nc.psum_tensor("pbc", [128, TOK], F32))
                po = esO.enter_context(nc.psum_tensor("po", [128, 1024], F32))

                @block.gpsimd
                def _(g):
                    if STAGE >= 5:
                        g.wait_ge(h_fs, s_fs.val)
                    s_ow.inc(g.dma_start(
                        out=wo16[:, :, :],
                        in_=w_o.rearrange("(dt p) c -> p dt c", p=128)), 16)
                    with g.register("quadO") as qd:
                        g.reg_load(qd, sids[0:1, 2:3])
                        for q in range(4):
                            with g.If_eq(qd, q):
                                g.dma_start(
                                    out=csum16[0:1, :, :],
                                    in_=colsum_sh.ap()[:, 1024 * q:1024 * (q + 1)].unsqueeze(0)
                                ).then_inc(h_ow, 16)
                    s_ow.val += 16

                for hb in range(NH):
                    @block.tensor
                    def _(t, hb=hb):
                        t.wait_ge(h_ow, s_ow.val)
                        if hb > 0:
                            t.wait_ge(h_oe, s_oe.val)
                        for ch in range(2):
                            ins = t.matmul(pbc[:, 512 * ch:512 * (ch + 1)],
                                           ones16[:, :],
                                           csum16[0:1, hb, 512 * ch:512 * (ch + 1)],
                                           start=True, stop=True)
                        s_op.inc(ins)

                    @block.vector
                    def _(v, hb=hb):
                        v.wait_ge(h_op, s_op.val)
                        ins = v.tensor_tensor(VT[:, hb, :], VT[:, hb, :],
                                              pbc[:, :], ALU.mult)
                        s_oe.inc(ins)

                for tt in range(8):
                    @block.sync
                    def _(sy, tt=tt):
                        if tt == 0 and STAGE >= 5:
                            sy.wait_ge(h_fs, s_fs.val)
                        if tt > 0:
                            sy.wait_ge(h_oc, s_oc.val)
                        s_ox.inc(sy.dma_start(out=xow[:, :],
                                              in_=x_own[128 * tt:128 * (tt + 1), :]), 16)

                    for dc in range(4):
                        idx = 4 * tt + dc

                        @block.tensor
                        def _(t, tt=tt, dc=dc, idx=idx):
                            if idx == 0:
                                t.wait_ge(h_oe, s_oe.val)
                            if idx >= 2:
                                t.wait_ge(h_oc, idx - 1)
                            par = idx % 2
                            for hb in range(NH):
                                ins = t.matmul(po[:, 512 * par:512 * (par + 1)],
                                               VT[:, hb, 128 * tt:128 * (tt + 1)],
                                               wo16[:, hb, 512 * dc:512 * (dc + 1)],
                                               start=(hb == 0), stop=(hb == NH - 1))
                            s_op.inc(ins)

                        @block.vector
                        def _(v, tt=tt, dc=dc, idx=idx):
                            v.wait_ge(h_op, s_op.val)
                            v.wait_ge(h_ox, s_ox.val)
                            par = idx % 2
                            s_chn.inc(v.tensor_tensor(Pb[:, :], po[:, 512 * par:512 * (par + 1)],
                                                      g_sb[:, tt, 512 * dc:512 * (dc + 1)],
                                                      ALU.mult))
                            v.wait_ge(h_ch, s_chn.val)
                            ins = v.tensor_tensor(obuf[:, 512 * dc:512 * (dc + 1)],
                                                  Pb[:, :], xow[:, 512 * dc:512 * (dc + 1)],
                                                  ALU.add)
                            s_oc.inc(ins)

                    @block.sync
                    def _(sy, tt=tt):
                        sy.wait_ge(h_oc, s_oc.val)
                        s_ox.inc(sy.dma_start(out=out[128 * tt:128 * (tt + 1), :],
                                              in_=obuf[:, :]), 16)

                @block.sync
                def _(sy):
                    sy.wait_ge(h_ox, s_ox.val)

        if STAGE == 1:
            @block.gpsimd
            def _(g):
                g.wait_ge(h_sync, s_sync.val)
                s_st.inc(g.dma_start(out=dbg, in_=xnT_sh.ap()[0:TOK, 0:D]), 16)
        elif STAGE == 5:
            @block.gpsimd
            def _(g):
                g.wait_ge(h_fs, s_fs.val)
                s_st.inc(g.dma_start(
                    out=dbg.rearrange("(tt p) d -> p tt d", p=128),
                    in_=g_sb[:, :, :]), 16)
        elif STAGE == 4:
            @block.gpsimd
            def _(g):
                g.wait_ge(h_sync, s_sync.val)
                s_st.inc(g.dma_start(out=dbg[0:1024, 0:128], in_=wqgu_blk[0, 0:1024, :]), 16)
                s_st.inc(g.dma_start(out=dbg[0:1024, 128:256], in_=wqgu_blk[7, 9000:10024, :]), 16)
                s_st.inc(g.dma_start(out=dbg[0:1024, 256:384], in_=wqp_blk[0, 0:1024, :]), 16)
                s_st.inc(g.dma_start(out=dbg[0:1024, 384:512], in_=wqp_blk[33, 1024:2048, :]), 16)
        elif STAGE == 3:
            @block.gpsimd
            def _(g):
                g.wait_ge(h_sync, s_sync.val)
                s_st.inc(g.dma_start(
                    out=dbg[0:32, 0:2048],
                    in_=colsum_sh.ap().rearrange("h (a t) -> (h a) t", a=2)), 16)
        elif STAGE == 35:
            @block.gpsimd
            def _(g):
                g.wait_ge(h_cst, s_cst.val)
                s_st.inc(g.dma_start(out=dbg[0:128, 0:32], in_=racc[:, :]), 16)
                s_st.inc(g.dma_start(out=dbg[0:128, 32:40], in_=rsum[:, :]), 16)
                s_st.inc(g.dma_start(out=dbg[0:128, 40:48], in_=arec[:, :]), 16)
                s_st.inc(g.dma_start(out=dbg[0:128, 48:56], in_=a16[:, :]), 16)
                s_st.inc(g.dma_start(out=dbg[128:256, 0:2048], in_=Egrp[:, 0, 0:2048]), 16)
                s_st.inc(g.dma_start(out=dbg[256:384, 0:2048], in_=Egrp[:, 0, 2048:4096]), 16)
                s_st.inc(g.dma_start(out=dbg[384:386, 0:2048],
                                     in_=colsum_sh.ap()[0:1, :].rearrange("h (a t) -> (h a) t", a=2)), 16)
                s_st.inc(g.dma_start(out=dbg[512:640, 0:2048], in_=QT[:, 0:2048]), 16)
                s_st.inc(g.dma_start(out=dbg[640:768, 0:2048], in_=KT[:, 0:2048]), 16)
                s_st.inc(g.dma_start(out=dbg[386:387, 0:2048],
                                     in_=cget.ap()[0:1, 0:4, :].rearrange("a g t -> a (g t)")), 16)
                s_st.inc(g.dma_start(out=dbg[387:388, 0:2048],
                                     in_=cget.ap()[0:1, 4:8, :].rearrange("a g t -> a (g t)")), 16)
                s_st.inc(g.dma_start(out=dbg[768:896, 0:32],
                                     in_=aget.ap().rearrange("p g j -> p (g j)")), 16)
                s_st.inc(g.dma_start(out=dbg[896:1024, 0:2048], in_=eget[:, :]), 16)
                s_st.inc(g.dma_start(out=dbg[768:896, 32:40], in_=rget[:, :]), 16)
                s_st.inc(g.dma_start(out=dbg[768:896, 40:72], in_=r2get[:, :]), 16)
        elif STAGE == 2:
            @block.gpsimd
            def _(g):
                g.wait_ge(h_qe, s_qe.val)
                s_st.inc(g.dma_start(out=dbg[0:128, 0:2048], in_=QT[:, 0, 0:2048]), 16)
                s_st.inc(g.dma_start(out=dbg[128:256, 0:2048], in_=QT[:, 0, 2048:4096]), 16)
                s_st.inc(g.dma_start(out=dbg[256:384, 0:2048], in_=KT[:, 0, 0:2048]), 16)
                s_st.inc(g.dma_start(out=dbg[384:512, 0:2048], in_=KT[:, 0, 2048:4096]), 16)
                s_st.inc(g.dma_start(out=dbg[512:640, 0:1024], in_=VT[:, 0, :]), 16)
                s_st.inc(g.dma_start(out=dbg[640:768, 0:1024], in_=VT[:, 9, :]), 16)

            @block.gpsimd
            def _(g):
                g.wait_ge(h_st, s_st.val)

    return nc


_NC_CACHE = None


def _in_maps(X, Wqkv, Wo, Wgu, Wp):
    in_maps, meta = [], []
    for c in range(8):
        m = c % 2
        pair = c // 2
        b = pair % 2
        quad = {0: 0, 1: 1, 4: 2, 5: 3, 2: 0, 3: 1, 6: 2, 7: 3}[c]
        own = 1024 * quad
        hs = 128 * HPM * m
        w_qk_c = np.concatenate(
            [Wqkv[:, hs:hs + 1024], Wqkv[:, 2048 + hs:2048 + hs + 1024]], axis=1)
        idv = np.zeros((1, 16), np.int32)
        idv[0, 0] = c
        idv[0, 1] = m
        idv[0, 2] = quad
        in_maps.append({
            "x_half": np.ascontiguousarray(X[b, HALF * m:HALF * (m + 1), :]),
            "x_own": np.ascontiguousarray(X[b, own:own + TOK, :]),
            "w_qk": np.ascontiguousarray(w_qk_c),
            "w_v": np.ascontiguousarray(Wqkv[:, 4096:6144]),
            "w_o": Wo,
            "wgu_half": np.ascontiguousarray(Wgu[HF * m:HF * (m + 1), :]),
            "wp_half": np.ascontiguousarray(Wp[1024 * m:1024 * (m + 1), :]),
            "ids": idv,
        })
        meta.append((b, own))
    return in_maps, meta


def kernel(L0_attn_x, L0_attn_w_qkv, L0_attn_w_o, L0_ffn_w_gate_up, L0_ffn_w_proj):
    global _NC_CACHE
    X = np.ascontiguousarray(np.asarray(L0_attn_x, dtype=np.float32))
    Wqkv = np.ascontiguousarray(np.asarray(L0_attn_w_qkv, dtype=np.float32))
    Wo = np.ascontiguousarray(np.asarray(L0_attn_w_o, dtype=np.float32))
    Wgu = np.ascontiguousarray(np.asarray(L0_ffn_w_gate_up, dtype=np.float32))
    Wp = np.ascontiguousarray(np.asarray(L0_ffn_w_proj, dtype=np.float32))

    if _NC_CACHE is None:
        _NC_CACHE = build_program()
    nc = _NC_CACHE

    in_maps, meta = _in_maps(X, Wqkv, Wo, Wgu, Wp)
    res = run_bass_kernel_spmd(nc, in_maps, core_ids=list(range(8)))
    outp = np.zeros((B, S, D), np.float32)
    for c in range(8):
        b, own = meta[c]
        outp[b, own:own + TOK, :] = res.results[c]["out"]
    if STAGE < 99:
        kernel.debug = [r["dbg"] for r in res.results]
    return outp


# version 11
# speedup vs baseline: 1.3538x; 1.3538x over previous
import numpy as np
from contextlib import ExitStack
import concourse.bass as bass
import concourse.mybir as mybir
from concourse.bass_utils import run_bass_kernel_spmd

F32 = mybir.dt.float32
BF16 = mybir.dt.bfloat16
I32 = mybir.dt.int32
AF = mybir.ActivationFunctionType
ALU = mybir.AluOpType
AX = mybir.AxisListType

B = 2
S = 4096
D = 2048
NH = 16
DH = 128
HF = 8192
TOK = 1024
HALF = 2048
HPM = 8
INV_SQRT_DH = float(np.float32(1.0) / np.sqrt(np.float32(DH)))
RMS_EPS = float(np.finfo(np.float32).eps)
LN_EPS = 1e-5
MAGIC = 19_750_623

DT = D // 128
NQT = S // 128
GRP = 4
NGRP = NQT // GRP
NCH = HALF // 512

STAGE = 99


class Sem:

    def __init__(self, handle):
        self.h = handle
        self.val = 0

    def inc(self, ins, k=1):
        ins.then_inc(self.h, k)
        self.val += k
        return self.val


def _flag_row(sync_id, member):
    return 2 * sync_id + member


def build_program(detect_races=True):
    nc = bass.Bass("TRN2", target_bir_lowering=False, num_devices=8,
                   detect_race_conditions=detect_races,
                   dynamic_dma_scratch_size=16384)

    x_half = nc.dram_tensor("x_half", [HALF, D], F32, kind="ExternalInput").ap()
    x_own = nc.dram_tensor("x_own", [TOK, D], F32, kind="ExternalInput").ap()
    w_qk = nc.dram_tensor("w_qk", [D, 2048], F32, kind="ExternalInput").ap()
    w_v = nc.dram_tensor("w_v", [D, 2048], F32, kind="ExternalInput").ap()
    w_o = nc.dram_tensor("w_o", [D, D], F32, kind="ExternalInput").ap()
    wgu_half = nc.dram_tensor("wgu_half", [HF, D], F32, kind="ExternalInput").ap()
    wp_half = nc.dram_tensor("wp_half", [1024, HF], F32, kind="ExternalInput").ap()
    ids = nc.dram_tensor("ids", [1, 16], I32, kind="ExternalInput").ap()
    out = nc.dram_tensor("out", [TOK, D], F32, kind="ExternalOutput").ap()

    xb16 = nc.dram_tensor("xb16", [DT, HALF, 128], BF16)
    xnT_sh = nc.dram_tensor("xnT_sh", [D, S], BF16, addr_space="Shared")
    wqgu_blk = nc.dram_tensor("wqgu_blk", [DT, 2 * HF, 128], BF16, addr_space="Shared")
    wqp_blk = nc.dram_tensor("wqp_blk", [HF // 128, D, 128], BF16, addr_space="Shared")
    colsum_sh = nc.dram_tensor("colsum_sh", [NH, S], F32, addr_space="Shared")
    flags_sh = nc.dram_tensor("flags_sh", [16, 16], I32, addr_space="Shared")
    hT_loc = nc.dram_tensor("hT_loc", [HF, TOK], BF16)
    if STAGE < 99:
        dbg = nc.dram_tensor("dbg", [TOK, D], F32, kind="ExternalOutput").ap()

    es = ExitStack()
    with es:
        sb = lambda *a, **k: es.enter_context(nc.sbuf_tensor(*a, **k))
        sem = lambda n: es.enter_context(nc.semaphore(n))
        sids = sb("sids", [1, 16], I32)
        ones16 = sb("ones16", [1, 128], BF16)
        onescol = sb("onescol", [128, 1], BF16)
        ones32 = sb("ones32", [1, 128], F32)
        smagic = sb("smagic", [1, 16], I32)
        consts = sb("consts", [128, 4], F32)
        sflag = sb("sflag", [1, 32], I32)
        h_in = sem("s_in"); h_ln = sem("s_ln"); h_lnv = sem("s_lnv")
        h_mm = sem("s_mm"); h_st = sem("s_st"); h_poll = sem("s_poll")
        h_sync = sem("s_sync")
        h_ch = sem("s_ch")
        h_tp = sem("s_tp")
        h_xst = sem("s_xst")
        block = es.enter_context(nc.Block())
        s_in, s_ln, s_lnv, s_mm, s_st = Sem(h_in), Sem(h_ln), Sem(h_lnv), Sem(h_mm), Sem(h_st)
        s_sync = Sem(h_sync)
        s_chn = Sem(h_ch)
        s_tp = Sem(h_tp)
        s_xst = Sem(h_xst)

        esA = ExitStack()
        with esA:
            sbA = lambda *a, **k: esA.enter_context(nc.sbuf_tensor(*a, **k))
            psA = lambda *a, **k: esA.enter_context(nc.psum_tensor(*a, **k))
            xt = sbA("xt", [128, DT, 512], BF16, is_transpose_dest=True)
            xt2 = sbA("xt2", [128, DT, 512], BF16)
            rowm = sbA("rowm", [1, 512], F32)
            rows2 = sbA("rows2", [1, 512], F32)
            rowtmp = sbA("rowtmp", [1, 512], F32)
            rowvar = sbA("rowvar", [1, 512], F32)
            rowrstd = sbA("rowrstd", [1, 512], F32)
            rowm16 = sbA("rowm16", [1, 512], BF16)
            rowrstd16 = sbA("rowrstd16", [1, 512], BF16)
            mbc = sbA("mbc", [128, 512], BF16)
            rbc = sbA("rbc", [128, 512], BF16)
            cen = sbA("cen", [128, DT, 512], BF16)
            xnt_c = sbA("xnt_c", [128, DT, 512], BF16)
            pS = psA("pS", [1, 512], F32)
            pS2 = psA("pS2", [1, 512], F32)
            pM = psA("pM", [128, 512], F32)
            pR = psA("pR", [128, 512], F32)
            @block.gpsimd
            def _(g):
                g.memset(smagic[:, :], MAGIC)
                s_in.inc(g.dma_start(out=sids[:, :], in_=ids), 16)
                xr = x_half.rearrange("t (dt dd) -> dt t dd", dd=128)
                for dt in range(DT):
                    s_in.inc(g.dma_start(out=xb16[dt, :, :], in_=xr[dt, :, :]), 16)

            @block.vector
            def _(v):
                v.memset(ones16[:, :], 1.0)
                v.memset(ones32[:, :], 1.0)
                v.memset(consts[:, 0:1], LN_EPS)
                v.memset(consts[:, 1:2], RMS_EPS)
                s_lnv.inc(v.memset(onescol[:, :], 1.0))

            for c in range(NCH):
                @block.sync
                def _(sy, c=c):
                    if c == 0:
                        sy.wait_ge(h_in, 272)
                    else:
                        sy.wait_ge(h_xst, s_xst.val)
                    for dt in range(DT):
                        ins = sy.dma_start_transpose(
                            xt[:, dt, :], xb16[dt, 512 * c:512 * (c + 1), :])
                        s_tp.inc(ins, 16)

                @block.vector
                def _(v, c=c):
                    v.wait_ge(h_tp, s_tp.val)
                    s_lnv.inc(v.tensor_tensor(xt2[:, :, :], xt[:, :, :], xt[:, :, :], ALU.mult))

                @block.tensor
                def _(t, c=c):
                    t.wait_ge(h_lnv, s_lnv.val)
                    for dt in range(DT):
                        t.matmul(pS[:, :], onescol[:, :], xt[:, dt, :],
                                 start=(dt == 0), stop=(dt == DT - 1))
                    for dt in range(DT):
                        ins = t.matmul(pS2[:, :], onescol[:, :], xt2[:, dt, :],
                                       start=(dt == 0), stop=(dt == DT - 1))
                    s_mm.inc(ins)

                @block.scalar
                def _(s, c=c):
                    s.wait_ge(h_mm, s_mm.val)
                    s.activation(rowm[:, :], pS[:, :], AF.Copy, scale=1.0 / D)
                    s_ln.inc(s.activation(rows2[:, :], pS2[:, :], AF.Copy, scale=1.0 / D))

                @block.vector
                def _(v, c=c):
                    v.wait_ge(h_ln, s_ln.val)
                    s_chn.inc(v.tensor_tensor(rowtmp[:, :], rowm[:, :], rowm[:, :], ALU.mult))
                    v.wait_ge(h_ch, s_chn.val)
                    s_lnv.inc(v.tensor_tensor(rowvar[:, :], rows2[:, :], rowtmp[:, :], ALU.subtract))

                @block.scalar
                def _(s, c=c):
                    s.wait_ge(h_lnv, s_lnv.val)
                    s_chn.inc(s.activation(rowtmp[:, :], rowvar[:, :], AF.Ln, bias=consts[0:1, 0:1]))
                    s.wait_ge(h_ch, s_chn.val)
                    s_ln.inc(s.activation(rowrstd[:, :], rowtmp[:, :], AF.Exp, scale=-0.5))

                @block.vector
                def _(v, c=c):
                    v.wait_ge(h_ln, s_ln.val)
                    v.tensor_copy(rowm16[:, :], rowm[:, :])
                    s_lnv.inc(v.tensor_copy(rowrstd16[:, :], rowrstd[:, :]))

                @block.tensor
                def _(t, c=c):
                    t.wait_ge(h_lnv, s_lnv.val)
                    t.matmul(pM[:, :], ones16[:, :], rowm16[:, :], start=True, stop=True)
                    s_mm.inc(t.matmul(pR[:, :], ones16[:, :], rowrstd16[:, :], start=True, stop=True))

                @block.vector
                def _(v, c=c):
                    v.wait_ge(h_mm, s_mm.val)
                    v.tensor_copy(mbc[:, :], pM[:, :])
                    s_chn.inc(v.tensor_copy(rbc[:, :], pR[:, :]))
                    v.wait_ge(h_ch, s_chn.val)
                    s_chn.inc(v.tensor_tensor(cen[:, :, :], xt[:, :, :], 
                                              mbc[:, :].unsqueeze(1).broadcast_to([128, DT, 512]), ALU.subtract))
                    v.wait_ge(h_ch, s_chn.val)
                    s_lnv.inc(v.tensor_tensor(xnt_c[:, :, :], cen[:, :, :],
                                              rbc[:, :].unsqueeze(1).broadcast_to([128, DT, 512]), ALU.mult))

                @block.gpsimd
                def _(g, c=c):
                    g.wait_ge(h_lnv, s_lnv.val)
                    with g.register(f"mem{c}") as mem:
                        g.reg_load(mem, sids[0:1, 1:2])
                        with g.If_eq(mem, 0):
                            g.dma_start(
                                out=xnT_sh.ap()[:, 512 * c:512 * (c + 1)].rearrange(
                                    "(dt p) t -> p dt t", p=128),
                                in_=xnt_c[:, :, :]).then_inc(h_xst, 16)
                        with g.Else():
                            g.dma_start(
                                out=xnT_sh.ap()[:, HALF + 512 * c:HALF + 512 * (c + 1)].rearrange(
                                    "(dt p) t -> p dt t", p=128),
                                in_=xnt_c[:, :, :]).then_inc(h_xst, 16)
                    s_xst.val += 16

            @block.gpsimd
            def _(g):
                g.wait_ge(h_xst, s_xst.val)
                with g.register("memS0") as mem:
                    g.reg_load(mem, sids[0:1, 1:2])
                    with g.If_eq(mem, 0):
                        g.dma_start(out=flags_sh[0:1, :], in_=smagic[:, :]).then_inc(h_poll, 16)
                    with g.Else():
                        g.dma_start(out=flags_sh[1:2, :], in_=smagic[:, :]).then_inc(h_poll, 16)
                g.wait_ge(h_poll, 16)
                with g.register("ndS0") as nd, g.register("tgtS0") as tgt:
                    g.reg_mov(nd, 1)
                    g.reg_mov(tgt, 16)
                    with g.While(nd):
                        g.reg_alu(tgt, tgt, 16, ALU.add)
                        g.dma_start(out=sflag[:, 0:32],
                                    in_=flags_sh.ap()[0:2, :].rearrange("a b -> (a b)").unsqueeze(0)
                                    ).then_inc(h_poll, 16)
                        g.wait_ge(h_poll, tgt)
                        with g.register("fv0S0") as fv0, g.register("fv1S0") as fv1:
                            g.reg_load(fv0, sflag[0:1, 0:1])
                            g.reg_load(fv1, sflag[0:1, 16:17])
                            g.reg_alu(fv0, fv0, MAGIC, ALU.not_equal)
                            g.reg_alu(fv1, fv1, MAGIC, ALU.not_equal)
                            g.reg_alu(nd, fv0, fv1, ALU.logical_or)
                s_sync.inc(g.nop())

        esB = ExitStack()
        with esB:
            sbB = lambda *a, **k: esB.enter_context(nc.sbuf_tensor(*a, **k))
            psB = lambda *a, **k: esB.enter_context(nc.psum_tensor(*a, **k))
            xnc2 = [sbB(f"xnc{i}", [128, DT, 512], BF16) for i in range(2)]
            xnc = xnc2[0]
            whead = sbB("whead", [128, DT, 256], BF16)
            QT = sbB("QT", [128, S], BF16)
            KT = sbB("KT", [128, S], BF16)
            racc = sbB("racc", [128, 8 * GRP], F32)
            rsum = sbB("rsum", [128, 2 * GRP], F32)
            arec = sbB("arec", [128, 2 * GRP], F32)
            a16 = sbB("a16", [128, 2 * GRP], BF16)
            cstage = sbB("cstage", [1, S], F32)
            if STAGE == 35:
                cget = sbB("cget", [1, 8, 512], F32)
                aget = sbB("aget", [128, 4, GRP], F32)
                eget = sbB("eget", [128, 2048], BF16)
                rget = sbB("rget", [128, GRP], F32)
                r2get = sbB("r2get", [128, GRP * 4], F32)
            pBig = [psB(f"pBig{i}", [128, 1024], F32) for i in range(4)]
            pQK = [pBig[i][:, 0:512] for i in range(4)]
            h_wl = es.enter_context(nc.semaphore("s_wl"))
            h_xc = es.enter_context(nc.semaphore("s_xc"))
            h_xcg = es.enter_context(nc.semaphore("s_xcg"))
            h_qp = es.enter_context(nc.semaphore("s_qp"))
            h_qe = es.enter_context(nc.semaphore("s_qe"))
            h_sc = es.enter_context(nc.semaphore("s_sc"))
            h_ex = es.enter_context(nc.semaphore("s_ex"))
            h_cse = es.enter_context(nc.semaphore("s_cse"))
            h_csb = es.enter_context(nc.semaphore("s_csb"))
            h_csv = es.enter_context(nc.semaphore("s_csv"))
            h_cst = es.enter_context(nc.semaphore("s_cst"))
            s_wl, s_xc, s_qp, s_qe = Sem(h_wl), Sem(h_xc), Sem(h_qp), Sem(h_qe)
            s_xcg = Sem(h_xcg)
            s_sc, s_ex, s_cse = Sem(h_sc), Sem(h_ex), Sem(h_cse)
            s_csb, s_csv, s_cst = Sem(h_csb), Sem(h_csv), Sem(h_cst)

            if STAGE == 35:
                @block.vector
                def _(v):
                    v.memset(cget[:, :, :], 0.0)
                    v.memset(eget[:, :], 0.0)
                    v.memset(rget[:, :], 0.0)
                    v.memset(r2get[:, :], 0.0)
                    v.memset(aget[:, :, :], 0.0)

            def emit_attention_head(hl, Egrp):
                @block.vector
                def _(v):
                    if s_cst.val > 0:
                        v.wait_ge(h_cst, s_cst.val)
                    v.memset(cstage[:, :], 0.0)

                cse0 = s_cse.val

                def emit_scores(g_i):
                    @block.tensor
                    def _(t, g_i=g_i):
                        if g_i == 0:
                            t.wait_ge(h_qe, s_qe.val)
                        for j in range(GRP):
                            i = g_i * GRP + j
                            for qtr in range(4):
                                q_idx = s_sc.val
                                if q_idx >= 2:
                                    t.wait_ge(h_ex, q_idx - 1)
                                par = q_idx % 2
                                for kc in range(2):
                                    ins = t.matmul(
                                        pBig[par][:, 512 * kc:512 * (kc + 1)],
                                        QT[:, 128 * i:128 * (i + 1)],
                                        KT[:, 1024 * qtr + 512 * kc:1024 * qtr + 512 * (kc + 1)],
                                        start=True, stop=True)
                                s_sc.inc(ins)

                    @block.scalar
                    def _(s, g_i=g_i):
                        base0 = s_sc.val - 4 * GRP
                        half = g_i % 2
                        for j in range(GRP):
                            for qtr in range(4):
                                base = base0 + 4 * j + qtr + 1
                                s.wait_ge(h_sc, base)
                                if j == 0 and qtr == 0 and g_i >= 2:
                                    s.wait_ge(h_cse, cse0 + g_i - 1)
                                par = (base - 1) % 2
                                ins = s.activation(
                                    Egrp[:, GRP * half + j, 1024 * qtr:1024 * (qtr + 1)],
                                    pBig[par][:, :], AF.Exp, scale=INV_SQRT_DH,
                                    accum_out=racc[:, 16 * half + 4 * j + qtr:16 * half + 4 * j + qtr + 1])
                                s_ex.inc(ins)

                    @block.vector
                    def _(v, g_i=g_i):
                        half = g_i % 2
                        v.wait_ge(h_ex, s_ex.val)
                        s_chn.inc(v.tensor_reduce(rsum[:, 4 * half:4 * (half + 1)],
                                        racc[:, 16 * half:16 * (half + 1)].rearrange(
                                            "p (g q) -> p g q", q=4),
                                        axis=AX.X, op=ALU.add))
                        v.wait_ge(h_ch, s_chn.val)
                        s_chn.inc(v.reciprocal(arec[:, 4 * half:4 * (half + 1)],
                                               rsum[:, 4 * half:4 * (half + 1)]))
                        v.wait_ge(h_ch, s_chn.val)
                        s_cse.inc(v.tensor_copy(a16[:, 4 * half:4 * (half + 1)],
                                                arec[:, 4 * half:4 * (half + 1)]))

                def emit_bursts(g_i):
                    half = g_i % 2

                    @block.tensor
                    def _(t, g_i=g_i, half=half):
                        t.wait_ge(h_cse, cse0 + g_i + 1)
                        for kc in range(8):
                            if s_csb.val >= 2:
                                t.wait_ge(h_csv, s_csb.val - 1)
                            pcs = pBig[2][0:1, 512 * (kc % 2):512 * (kc % 2 + 1)]
                            for j in range(GRP):
                                ins = t.matmul(pcs, a16[:, 4 * half + j:4 * half + j + 1],
                                               Egrp[:, GRP * half + j, 512 * kc:512 * (kc + 1)],
                                               start=(j == 0), stop=(j == GRP - 1))
                            s_csb.inc(ins)

                    @block.vector
                    def _(v, g_i=g_i):
                        base0 = s_csb.val - 8
                        for kc in range(8):
                            v.wait_ge(h_csb, base0 + kc + 1)
                            pcs = pBig[2][0:1, 512 * (kc % 2):512 * (kc % 2 + 1)]
                            ins = v.scalar_tensor_tensor(
                                cstage[0:1, 512 * kc:512 * (kc + 1)],
                                pcs, 1.0,
                                cstage[0:1, 512 * kc:512 * (kc + 1)],
                                ALU.mult, ALU.add)
                            s_csv.inc(ins)

                emit_scores(0)
                for g_i in range(1, NGRP):
                    emit_scores(g_i)
                    emit_bursts(g_i - 1)
                emit_bursts(NGRP - 1)

                @block.gpsimd
                def _(g, hl=hl):
                    g.wait_ge(h_csv, s_csv.val)
                    with g.register(f"memH{hl}") as mem:
                        g.reg_load(mem, sids[0:1, 1:2])
                        with g.If_eq(mem, 0):
                            g.dma_start(out=colsum_sh[hl:hl + 1, :],
                                        in_=cstage[:, :]).then_inc(h_cst, 16)
                        with g.Else():
                            g.dma_start(out=colsum_sh[HPM + hl:HPM + hl + 1, :],
                                        in_=cstage[:, :]).then_inc(h_cst, 16)
                    s_cst.val += 16

            h_qw = es.enter_context(nc.semaphore("s_qw"))
            h_qs = es.enter_context(nc.semaphore("s_qs"))
            h_qv = es.enter_context(nc.semaphore("s_qv"))
            s_qw, s_qs, s_qv = Sem(h_qw), Sem(h_qs), Sem(h_qv)
            QW = 1024
            h_qg = es.enter_context(nc.semaphore("s_qg"))
            s_qg = Sem(h_qg)
            qD = [{
                "w16": sbB(f"qw16_{i}", [128, QW], BF16),
                "t16": sbB(f"qt16_{i}", [128, QW], BF16),
                "tmp1": sbB(f"qtmp1_{i}", [128, QW], BF16),
                "tmp2": sbB(f"qtmp2_{i}", [128, QW], BF16),
                "gsum": sbB(f"qgsum_{i}", [128, QW // 64], F32),
                "scl": sbB(f"qscl_{i}", [128, QW // 64], BF16),
                "rcp": sbB(f"qrcp_{i}", [128, QW // 64], F32),
            } for i in range(2)]

            def quant_tile_emitter(ti, kind, r, cc):
                G = QW // 64
                B = qD[ti % 2]
                qw16, qt16, qtmp1, qtmp2 = B["w16"], B["t16"], B["tmp1"], B["tmp2"]
                qgsum, qscl, qrcp = B["gsum"], B["scl"], B["rcp"]

                @block.gpsimd
                def _(g):
                    if ti > 1:
                        g.wait_ge(h_qv, ti - 1)
                    if kind == "gu":
                        ins = g.dma_start(out=qw16[:, :],
                                          in_=wgu_half[128 * r:128 * (r + 1),
                                                       QW * cc:QW * (cc + 1)])
                    else:
                        ins = g.dma_start(out=qw16[:, :],
                                          in_=wp_half[128 * r:128 * (r + 1),
                                                      QW * cc:QW * (cc + 1)])
                    s_qw.inc(ins, 16)

                qw_target = s_qw.val

                @block.vector
                def _(v):
                    v.wait_ge(h_qw, qw_target)
                    w3 = qw16[:, :].rearrange("p (g s) -> p g s", s=64)
                    s_chn.inc(v.tensor_reduce(qgsum[:, :], w3, axis=AX.X,
                                              op=ALU.add, apply_absolute_value=True))
                    v.wait_ge(h_ch, s_chn.val)
                    s_chn.inc(v.tensor_scalar(qscl[:, :], qgsum[:, :],
                                              1.0 / 64, 1e-8, ALU.mult, ALU.max))
                    v.wait_ge(h_ch, s_chn.val)
                    s_chn.inc(v.reciprocal(qrcp[:, :], qscl[:, :]))

                @block.gpsimd
                def _(g):
                    g.wait_ge(h_ch, s_chn.val)
                    rb = qrcp[:, :].unsqueeze(2).broadcast_to([128, G, 64])
                    ins = g.tensor_tensor(
                        qt16[:, :].rearrange("p (g s) -> p g s", s=64),
                        qw16[:, :].rearrange("p (g s) -> p g s", s=64), rb, ALU.mult)
                    s_qg.inc(ins)

                @block.vector
                def _(v):
                    v.wait_ge(h_qg, s_qg.val)
                    s_chn.inc(v.tensor_scalar(qtmp1[:, :], qt16[:, :], 0.5, None, ALU.is_gt))
                    s_chn.inc(v.tensor_scalar(qtmp2[:, :], qt16[:, :], -0.5, None, ALU.is_lt))
                    v.wait_ge(h_ch, s_chn.val)
                    s_chn.inc(v.tensor_tensor(qt16[:, :], qtmp1[:, :], qtmp2[:, :], ALU.subtract))

                @block.gpsimd
                def _(g):
                    g.wait_ge(h_ch, s_chn.val)
                    sb_ = qscl[:, :].unsqueeze(2).broadcast_to([128, G, 64])
                    ins = g.tensor_tensor(
                        qtmp1[:, :].rearrange("p (g s) -> p g s", s=64),
                        qt16[:, :].rearrange("p (g s) -> p g s", s=64), sb_, ALU.mult)
                    s_qg.inc(ins)

                @block.vector
                def _(v):
                    v.wait_ge(h_qg, s_qg.val)
                    s_chn.inc(v.tensor_tensor(qtmp2[:, :], qtmp1[:, :], qw16[:, :], ALU.subtract))
                    v.wait_ge(h_ch, s_chn.val)
                    if ti > 1:
                        v.wait_ge(h_qs, 16 * (ti - 1))
                    s_chn.inc(v.tensor_tensor(qtmp1[:, :], qw16[:, :], qtmp2[:, :], ALU.add))
                    v.wait_ge(h_ch, s_chn.val)
                    s_qv.inc(v.nop())

                @block.gpsimd
                def _(g):
                    g.wait_ge(h_qv, ti + 1)
                    with g.register(f"memQT{ti}") as mem:
                        g.reg_load(mem, sids[0:1, 1:2])
                        for mm_ in range(2):
                            nsub = QW // 128
                            if kind == "gu":
                                dst0 = HF * mm_ + 128 * r
                                blk = wqgu_blk
                            else:
                                dst0 = 1024 * mm_ + 128 * r
                                blk = wqp_blk
                            with (g.If_eq(mem, 0) if mm_ == 0 else g.Else()):
                                g.dma_start(
                                    out=blk.ap()[nsub * cc:nsub * (cc + 1), dst0:dst0 + 128, :].rearrange(
                                        "dt p dd -> p dt dd"),
                                    in_=qtmp1[:, :].rearrange("p (dt dd) -> p dt dd", dd=128)
                                ).then_inc(h_qs, 16)
                    s_qs.val += 16

            _qtiles = ([("gu", r, c) for r in range(64) for c in range(2048 // QW)]
                       + [("p", r, c) for r in range(8) for c in range(HF // QW)])
            _qstate = {"i": 0}

            def emit_quant(k):
                while k > 0 and _qstate["i"] < len(_qtiles):
                    kind, r, c = _qtiles[_qstate["i"]]
                    quant_tile_emitter(_qstate["i"], kind, r, c)
                    _qstate["i"] += 1
                    k -= 1

            def flush_quant():
                emit_quant(10 ** 9)

            esC = ExitStack()
            with esC:
                Egrp = esC.enter_context(nc.sbuf_tensor("Egrp", [128, 2 * GRP, S], BF16))
                NB = 1 if STAGE in (2, 35) else HPM
                for hl in range(NB):
                    @block.gpsimd
                    def _(g, hl=hl):
                        if hl > 0:
                            g.wait_ge(h_qe, s_qe.val)
                        for qk in range(2):
                            cs = 1024 * qk + 128 * hl
                            ins = g.dma_start(
                                out=whead[:, :, 128 * qk:128 * (qk + 1)],
                                in_=w_qk[:, cs:cs + 128].rearrange(
                                    "(dt p) c -> p dt c", p=128))
                            s_wl.inc(ins, 16)

                    qp_base = s_qp.val
                    qe_base = s_qe.val

                    @block.sync
                    def _(sy, hl=hl, qp_base=qp_base):
                        sy.wait_ge(h_sync, s_sync.val)
                        for c in range(S // 512):
                            if hl > 0 or c >= 2:
                                sy.wait_ge(h_qp, qp_base + max(0, c - 1))
                            s_xc.inc(sy.dma_start(
                                out=xnc2[c % 2][:, :, :],
                                in_=xnT_sh.ap()[:, 512 * c:512 * (c + 1)].rearrange(
                                    "(dt p) t -> p dt t", p=128)), 16)
                    sxc_base = s_xc.val - 16 * 8

                    @block.tensor
                    def _(t, hl=hl, sxc_base=sxc_base, qe_base=qe_base):
                        t.wait_ge(h_wl, s_wl.val)
                        if s_csv.val > 0:
                            t.wait_ge(h_csv, s_csv.val)
                        for c in range(S // 512):
                            t.wait_ge(h_xc, sxc_base + 16 * (c + 1))
                            if c >= 2:
                                t.wait_ge(h_qe, qe_base + c - 1)
                            for qk in range(2):
                                for dt in range(DT):
                                    ins = t.matmul(
                                        pQK[2 * (c % 2) + qk],
                                        whead[:, dt, 128 * qk:128 * (qk + 1)],
                                        xnc2[c % 2][:, dt, :],
                                        start=(dt == 0), stop=(dt == DT - 1))
                                if qk == 1:
                                    s_qp.inc(ins)

                    @block.vector
                    def _(v, hl=hl):
                        base0 = s_qp.val - 8
                        for c in range(S // 512):
                            v.wait_ge(h_qp, base0 + c + 1)
                            v.tensor_copy(QT[:, 512 * c:512 * (c + 1)], pQK[2 * (c % 2)])
                            ins = v.tensor_copy(KT[:, 512 * c:512 * (c + 1)], pQK[2 * (c % 2) + 1])
                            s_qe.inc(ins)

                    if STAGE >= 3:
                        emit_attention_head(hl, Egrp)

                if STAGE >= 4:
                    chn_before = s_chn.val
                    qv_before = s_qv.val
                    flush_quant()

                    @block.tensor
                    def _(t):
                        for k in range(chn_before + 2, s_chn.val, 2):
                            t.wait_ge(h_ch, k)
                            t.matmul(pBig[3][0:1, 0:512], onescol[:, :],
                                     QT[:, 0:512], start=True, stop=True)

        VT = es.enter_context(nc.sbuf_tensor("VT", [128, NH, TOK], BF16))
        esV = ExitStack()
        with esV:
            wv16 = esV.enter_context(nc.sbuf_tensor("wv16", [128, DT, 2048], BF16))
            xncV = esV.enter_context(nc.sbuf_tensor("xncV", [128, DT, 512], BF16))
            pVP = [esV.enter_context(nc.psum_tensor(f"pVP{i}", [128, 512], F32)) for i in range(4)]

            @block.gpsimd
            def _(g):
                g.wait_ge(h_qe, s_qe.val)
                if STAGE >= 3:
                    g.wait_ge(h_cst, s_cst.val)
                s_wl.inc(g.dma_start(
                    out=wv16[:, :, :],
                    in_=w_v.rearrange("(dt p) c -> p dt c", p=128)), 16)

            for oc in range(TOK // 512):
                @block.gpsimd
                def _(g, oc=oc):
                    g.wait_ge(h_wl, s_wl.val)
                    if oc > 0:
                        g.wait_ge(h_qe, s_qe.val)
                    with g.register(f"quadV{oc}") as qd:
                        g.reg_load(qd, sids[0:1, 2:3])
                        for q in range(4):
                            with g.If_eq(qd, q):
                                cs = 1024 * q + 512 * oc
                                g.dma_start(
                                    out=xncV[:, :, :],
                                    in_=xnT_sh.ap()[:, cs:cs + 512].rearrange(
                                        "(dt p) t -> p dt t", p=128)).then_inc(h_xcg, 16)
                    s_xcg.val += 16

                for grp in range(4):
                    @block.tensor
                    def _(t, oc=oc, grp=grp):
                        t.wait_ge(h_xcg, s_xcg.val)
                        if oc == 0 and grp == 0 and s_csv.val > 0:
                            t.wait_ge(h_csv, s_csv.val)
                        if oc > 0 or grp > 0:
                            t.wait_ge(h_qe, s_qe.val)
                        for h2 in range(4):
                            hh = 4 * grp + h2
                            for dt in range(DT):
                                ins = t.matmul(pVP[h2][:, :],
                                               wv16[:, dt, 128 * hh:128 * (hh + 1)],
                                               xncV[:, dt, :],
                                               start=(dt == 0), stop=(dt == DT - 1))
                        s_qp.inc(ins)

                    @block.vector
                    def _(v, oc=oc, grp=grp):
                        v.wait_ge(h_qp, s_qp.val)
                        for h2 in range(4):
                            hh = 4 * grp + h2
                            ins = v.tensor_copy(VT[:, hh, 512 * oc:512 * (oc + 1)],
                                                pVP[h2][:, :])
                        s_qe.inc(ins)

        if STAGE >= 3:
            h_poll1 = es.enter_context(nc.semaphore("s_poll1"))

            @block.gpsimd
            def _(g):
                g.wait_ge(h_cst, s_cst.val)
                with g.register("memC") as mem:
                    g.reg_load(mem, sids[0:1, 1:2])
                    with g.If_eq(mem, 0):
                        g.dma_start(out=flags_sh[2:3, :], in_=smagic[:, :]).then_inc(h_poll1, 16)
                    with g.Else():
                        g.dma_start(out=flags_sh[3:4, :], in_=smagic[:, :]).then_inc(h_poll1, 16)
                g.wait_ge(h_poll1, 16)
                with g.register("ndC") as nd, g.register("tgtC") as tgt:
                    g.reg_mov(nd, 1)
                    g.reg_mov(tgt, 16)
                    with g.While(nd):
                        g.reg_alu(tgt, tgt, 16, ALU.add)
                        g.dma_start(out=sflag[:, 0:32],
                                    in_=flags_sh.ap()[2:4, :].rearrange("a b -> (a b)").unsqueeze(0)
                                    ).then_inc(h_poll1, 16)
                        g.wait_ge(h_poll1, tgt)
                        with g.register("fv0C") as fv0, g.register("fv1C") as fv1:
                            g.reg_load(fv0, sflag[0:1, 0:1])
                            g.reg_load(fv1, sflag[0:1, 16:17])
                            g.reg_alu(fv0, fv0, MAGIC, ALU.not_equal)
                            g.reg_alu(fv1, fv1, MAGIC, ALU.not_equal)
                            g.reg_alu(nd, fv0, fv1, ALU.logical_or)
                s_sync.inc(g.nop())

        if STAGE >= 4:
            h_poll2 = es.enter_context(nc.semaphore("s_poll2"))

            @block.gpsimd
            def _(g):
                g.wait_ge(h_qs, s_qs.val)
                with g.register("memW") as mem:
                    g.reg_load(mem, sids[0:1, 1:2])
                    with g.If_eq(mem, 0):
                        g.dma_start(out=flags_sh[4:5, :], in_=smagic[:, :]).then_inc(h_poll2, 16)
                    with g.Else():
                        g.dma_start(out=flags_sh[5:6, :], in_=smagic[:, :]).then_inc(h_poll2, 16)
                g.wait_ge(h_poll2, 16)
                with g.register("ndW") as nd, g.register("tgtW") as tgt:
                    g.reg_mov(nd, 1)
                    g.reg_mov(tgt, 16)
                    with g.While(nd):
                        g.reg_alu(tgt, tgt, 16, ALU.add)
                        g.dma_start(out=sflag[:, 0:32],
                                    in_=flags_sh.ap()[4:6, :].rearrange("a b -> (a b)").unsqueeze(0)
                                    ).then_inc(h_poll2, 16)
                        g.wait_ge(h_poll2, tgt)
                        with g.register("fv0W") as fv0, g.register("fv1W") as fv1:
                            g.reg_load(fv0, sflag[0:1, 0:1])
                            g.reg_load(fv1, sflag[0:1, 16:17])
                            g.reg_alu(fv0, fv0, MAGIC, ALU.not_equal)
                            g.reg_alu(fv1, fv1, MAGIC, ALU.not_equal)
                            g.reg_alu(nd, fv0, fv1, ALU.logical_or)
                s_sync.inc(g.nop())

        if STAGE >= 5:
            g_sb = es.enter_context(nc.sbuf_tensor("g_sb", [128, TOK // 128, D], BF16))
            h_fw = es.enter_context(nc.semaphore("s_fw"))
            h_fx = es.enter_context(nc.semaphore("s_fx"))
            h_fp = es.enter_context(nc.semaphore("s_fp"))
            h_fe = es.enter_context(nc.semaphore("s_fe"))
            h_fh = es.enter_context(nc.semaphore("s_fh"))
            h_fs = es.enter_context(nc.semaphore("s_fs"))
            s_fw, s_fx, s_fp, s_fe = Sem(h_fw), Sem(h_fx), Sem(h_fp), Sem(h_fe)
            s_fh, s_fs = Sem(h_fh), Sem(h_fs)

            esF = ExitStack()
            with esF:
                sbF = lambda *a, **k: esF.enter_context(nc.sbuf_tensor(*a, **k))
                psF = lambda *a, **k: esF.enter_context(nc.psum_tensor(*a, **k))
                inv_col = sbF("inv_col", [128, 8], F32)
                pcolF = psF("pcolF", [128, 8], F32)

                h_fpg = es.enter_context(nc.semaphore("s_fpg"))
                h_fpu = es.enter_context(nc.semaphore("s_fpu"))
                h_feu = es.enter_context(nc.semaphore("s_feu"))
                h_f2 = es.enter_context(nc.semaphore("s_f2"))
                h_fpz = es.enter_context(nc.semaphore("s_fpz"))
                s_fpg, s_fpu, s_feu, s_f2 = Sem(h_fpg), Sem(h_fpu), Sem(h_feu), Sem(h_f2)
                s_fpz = Sem(h_fpz)

                esG = ExitStack()
                with esG:
                    xno = esG.enter_context(nc.sbuf_tensor("xno", [128, DT, TOK], BF16))
                    invrow = esG.enter_context(nc.sbuf_tensor("invrow", [1, TOK], F32))
                    hsrow = esG.enter_context(nc.sbuf_tensor("hsrow", [1, TOK], F32))

                    @block.gpsimd
                    def _(g):
                        g.wait_ge(h_sync, s_sync.val)
                        g.wait_ge(h_qe, s_qe.val)
                        with g.register("quadF") as qd:
                            g.reg_load(qd, sids[0:1, 2:3])
                            for q in range(4):
                                with g.If_eq(qd, q):
                                    g.dma_start(
                                        out=xno[:, :, :],
                                        in_=xnT_sh.ap()[:, 1024 * q:1024 * (q + 1)].rearrange(
                                            "(dt p) t -> p dt t", p=128)).then_inc(h_fx, 16)
                        s_fx.val += 16

                    gat = [esG.enter_context(nc.sbuf_tensor(f"gat{i}", [128, DT, 512], BF16)) for i in range(2)]
                    upt = [esG.enter_context(nc.sbuf_tensor(f"upt{i}", [128, DT, 512], BF16)) for i in range(2)]
                    sgb = esG.enter_context(nc.sbuf_tensor("sgb", [128, 1024], BF16))
                    upe = esG.enter_context(nc.sbuf_tensor("upe", [128, 1024], BF16))
                    hTt = [esG.enter_context(nc.sbuf_tensor(f"hTt{i}", [128, 1024], BF16)) for i in range(2)]
                    h2t = esG.enter_context(nc.sbuf_tensor("h2t", [128, 1024], BF16))
                    h2row = esG.enter_context(nc.sbuf_tensor("h2row", [1, TOK], F32))
                    pg = esG.enter_context(nc.psum_tensor("pg", [128, 1024], F32))
                    pu = esG.enter_context(nc.psum_tensor("pu", [128, 1024], F32))
                    ph2 = esG.enter_context(nc.psum_tensor("ph2", [1, 1024], F32))

                    @block.vector
                    def _(v):
                        v.memset(h2row[:, :], 0.0)

                    NGF = 16
                    fpg0, fpu0, fs0, fe0, feu0, fh0, f20 = (s_fpg.val, s_fpu.val,
                        s_fs.val, s_fe.val, s_feu.val, s_fh.val, s_f2.val)

                    for hg in range(NGF):
                        @block.sync
                        def _(sy, hg=hg):
                            if hg == 0:
                                sy.wait_ge(h_sync, s_sync.val)
                                sy.wait_ge(h_qe, s_qe.val)
                                sy.wait_ge(h_cst, s_cst.val)
                            if hg >= 2:
                                sy.wait_ge(h_fpu, fpu0 + 4 * (hg - 1))
                            for dt in range(DT):
                                sy.dma_start_transpose(
                                    gat[hg % 2][:, dt, :],
                                    wqgu_blk[dt, 512 * hg:512 * (hg + 1), :]
                                ).then_inc(h_fw, 16)
                                sy.dma_start_transpose(
                                    upt[hg % 2][:, dt, :],
                                    wqgu_blk[dt, HF + 512 * hg:HF + 512 * (hg + 1), :]
                                ).then_inc(h_fw, 16)
                            s_fw.val += 32 * DT

                        def emit_h2_mm(t, ht):
                            htl = ht % 4
                            hgl = ht // 4
                            t.wait_ge(h_fe, fe0 + ht + 1)
                            if htl == 0 and hgl >= 1:
                                t.wait_ge(h_fpz, hgl)
                            for ch in range(2):
                                ins = t.matmul(ph2[0:1, 512 * ch:512 * (ch + 1)],
                                               onescol[:, :],
                                               h2t[:, 512 * ch:512 * (ch + 1)],
                                               start=(htl == 0), stop=(htl == 3))
                            s_f2.inc(ins)

                        @block.tensor
                        def _(t, hg=hg):
                            t.wait_ge(h_fw, s_fw.val)
                            if hg == 0:
                                t.wait_ge(h_fx, s_fx.val)
                            for htl in range(4):
                                ht = 4 * hg + htl
                                if ht > 0:
                                    t.wait_ge(h_fs, fs0 + ht)
                                for ch in range(2):
                                    for dt in range(DT):
                                        ins = t.matmul(pg[:, 512 * ch:512 * (ch + 1)],
                                                 gat[hg % 2][:, dt, 128 * htl:128 * (htl + 1)],
                                                 xno[:, dt, 512 * ch:512 * (ch + 1)],
                                                 start=(dt == 0), stop=(dt == DT - 1))
                                s_fpg.inc(ins)
                                if ht > 0:
                                    t.wait_ge(h_feu, feu0 + ht)
                                for ch in range(2):
                                    for dt in range(DT):
                                        ins = t.matmul(pu[:, 512 * ch:512 * (ch + 1)],
                                                       upt[hg % 2][:, dt, 128 * htl:128 * (htl + 1)],
                                                       xno[:, dt, 512 * ch:512 * (ch + 1)],
                                                       start=(dt == 0), stop=(dt == DT - 1))
                                s_fpu.inc(ins)
                                if ht >= 1:
                                    emit_h2_mm(t, ht - 1)
                                if ht == 63:
                                    emit_h2_mm(t, 63)

                        @block.scalar
                        def _(s, hg=hg):
                            for htl in range(4):
                                ht = 4 * hg + htl
                                s.wait_ge(h_fpg, fpg0 + ht + 1)
                                if ht >= 1:
                                    s.wait_ge(h_fe, fe0 + ht)
                                ins = s.activation(sgb[:, :], pg[:, :], AF.Silu)
                                s_fs.inc(ins)

                        @block.vector
                        def _(v, hg=hg):
                            for htl in range(4):
                                ht = 4 * hg + htl
                                v.wait_ge(h_fpu, fpu0 + ht + 1)
                                if ht >= 1:
                                    v.wait_ge(h_fe, fe0 + ht)
                                s_feu.inc(v.tensor_copy(upe[:, :], pu[:, :]))
                                v.wait_ge(h_fs, fs0 + ht + 1)
                                v.wait_ge(h_feu, s_feu.val)
                                if ht >= 2:
                                    v.wait_ge(h_fh, fh0 + 16 * (ht - 1))
                                s_chn.inc(v.tensor_tensor(hTt[ht % 2][:, :], sgb[:, :],
                                                          upe[:, :], ALU.mult))
                                v.wait_ge(h_ch, s_chn.val)
                                if ht >= 1:
                                    v.wait_ge(h_f2, f20 + ht)
                                s_chn.inc(v.tensor_tensor(h2t[:, :], hTt[ht % 2][:, :],
                                                          hTt[ht % 2][:, :], ALU.mult))
                                v.wait_ge(h_ch, s_chn.val)
                                s_fe.inc(v.nop())

                        @block.vector
                        def _(v, hg=hg):
                            v.wait_ge(h_f2, f20 + 4 * (hg + 1))
                            for ch in range(2):
                                ins = v.scalar_tensor_tensor(
                                    h2row[0:1, 512 * ch:512 * (ch + 1)],
                                    ph2[0:1, 512 * ch:512 * (ch + 1)], 1.0,
                                    h2row[0:1, 512 * ch:512 * (ch + 1)],
                                    ALU.mult, ALU.add)
                            s_chn.inc(ins)
                            v.wait_ge(h_ch, s_chn.val)
                            s_fpz.inc(v.nop())

                        @block.gpsimd
                        def _(g, hg=hg):
                            for htl in range(4):
                                ht = 4 * hg + htl
                                g.wait_ge(h_fe, fe0 + ht + 1)
                                g.dma_start(out=hT_loc[128 * ht:128 * (ht + 1), :],
                                            in_=hTt[ht % 2][:, :]).then_inc(h_fh, 16)
                                s_fh.val += 16


                    @block.vector
                    def _(v):
                        v.wait_ge(h_f2, s_f2.val)
                        s_chn.inc(v.tensor_copy(hsrow[:, :], h2row[:, :]))

                    @block.scalar
                    def _(s):
                        s.wait_ge(h_ch, s_chn.val)
                        s_chn.inc(s.activation(invrow[:, :], hsrow[:, :], AF.Ln,
                                               scale=1.0 / HF, bias=consts[0:1, 1:2]))
                        s.wait_ge(h_ch, s_chn.val)
                        s_chn.inc(s.activation(invrow[:, :], invrow[:, :], AF.Exp, scale=-0.5))

                    @block.tensor
                    def _(t):
                        t.wait_ge(h_ch, s_chn.val)
                        for tt in range(8):
                            ins = t.matmul(pcolF[:, tt:tt + 1],
                                           invrow[0:1, 128 * tt:128 * (tt + 1)],
                                           ones32[0:1, 0:1], start=True, stop=True)
                        s_fp.inc(ins)

                    @block.vector
                    def _(v):
                        v.wait_ge(h_fp, s_fp.val)
                        s_fe.inc(v.tensor_copy(inv_col[:, :], pcolF[:, :]))

                esP = ExitStack()
                with esP:
                    wpqH = [esP.enter_context(nc.sbuf_tensor(f"wpqH{i}", [128, 32, 512], BF16))
                            for i in range(3)]
                    hTl = [esP.enter_context(nc.sbuf_tensor(f"hTl{i}", [128, 64, 128], BF16))
                           for i in range(2)]
                    pp = esP.enter_context(nc.psum_tensor("pp", [128, 1024], F32))
                    fpP0 = s_fp.val
                    fhP0 = s_fh.val
                    fwP0 = s_fw.val

                    for k in range(8):
                        dq, hh = k // 2, k % 2

                        @block.sync
                        def _(sy, k=k, dq=dq, hh=hh):
                            if k == 0:
                                sy.wait_ge(h_fe, s_fe.val)
                            if k >= 3:
                                sy.wait_ge(h_fp, fpP0 + 8 * ((k - 3) // 2 + 1))
                            for htl in range(32):
                                ht = 32 * hh + htl
                                sy.dma_start_transpose(
                                    wpqH[k % 3][:, htl, :],
                                    wqp_blk[ht, 512 * dq:512 * (dq + 1), :]
                                ).then_inc(h_fw, 16)
                            s_fw.val += 32 * 16

                    for dq in range(4):
                        for tt in range(8):
                            idx = 8 * dq + tt

                            @block.gpsimd
                            def _(g, dq=dq, tt=tt, idx=idx):
                                if idx == 0:
                                    g.wait_ge(h_fe, s_fe.val)
                                if idx >= 2:
                                    g.wait_ge(h_fp, fpP0 + idx - 1)
                                g.dma_start(
                                    out=hTl[idx % 2][:, :, :],
                                    in_=hT_loc.ap()[:, 128 * tt:128 * (tt + 1)].rearrange(
                                        "(ht p) t -> p ht t", p=128)).then_inc(h_fh, 16)
                                s_fh.val += 16

                            @block.tensor
                            def _(t, dq=dq, tt=tt, idx=idx):
                                t.wait_ge(h_fh, fhP0 + 16 * (idx + 1))
                                t.wait_ge(h_fw, fwP0 + 512 * (2 * dq + 2))
                                par = idx % 2
                                if idx >= 2:
                                    t.wait_ge(h_fs, s_fs.val - 1)
                                for ht in range(64):
                                    ins = t.matmul(pp[:, 512 * par:512 * (par + 1)],
                                                   hTl[idx % 2][:, ht, :],
                                                   wpqH[(2 * dq + ht // 32) % 3][:, ht % 32, :],
                                                   start=(ht == 0), stop=(ht == 63))
                                s_fp.inc(ins)

                        @block.scalar
                        def _(s, dq=dq):
                            base0 = s_fp.val - 8
                            for tt in range(8):
                                s.wait_ge(h_fp, base0 + tt + 1)
                                par = (8 * dq + tt) % 2
                                ins = s.activation(g_sb[:, tt, 512 * dq:512 * (dq + 1)],
                                                   pp[:, 512 * par:512 * (par + 1)],
                                                   AF.Silu, scale=inv_col[:, tt:tt + 1])
                                s_fs.inc(ins)

        if STAGE >= 6:
            h_ow = es.enter_context(nc.semaphore("s_ow"))
            h_op = es.enter_context(nc.semaphore("s_op"))
            h_oe = es.enter_context(nc.semaphore("s_oe"))
            h_oc = es.enter_context(nc.semaphore("s_oc"))
            h_ox = es.enter_context(nc.semaphore("s_ox"))
            s_ow, s_op, s_oe = Sem(h_ow), Sem(h_op), Sem(h_oe)
            s_oc, s_ox = Sem(h_oc), Sem(h_ox)

            esO = ExitStack()
            with esO:
                sbO = lambda *a, **k: esO.enter_context(nc.sbuf_tensor(*a, **k))
                wo16 = sbO("wo16", [128, DT, D], BF16)
                csum16 = sbO("csum16", [1, NH, TOK], BF16)
                xow = sbO("xow", [128, D], F32)
                Pb = sbO("Pb", [128, 512], F32)
                obuf = sbO("obuf", [128, D], F32)
                pbc = esO.enter_context(nc.psum_tensor("pbc", [128, TOK], F32))
                po = esO.enter_context(nc.psum_tensor("po", [128, 1024], F32))

                @block.gpsimd
                def _(g):
                    if STAGE >= 5:
                        g.wait_ge(h_fs, s_fs.val)
                    s_ow.inc(g.dma_start(
                        out=wo16[:, :, :],
                        in_=w_o.rearrange("(dt p) c -> p dt c", p=128)), 16)
                    with g.register("quadO") as qd:
                        g.reg_load(qd, sids[0:1, 2:3])
                        for q in range(4):
                            with g.If_eq(qd, q):
                                g.dma_start(
                                    out=csum16[0:1, :, :],
                                    in_=colsum_sh.ap()[:, 1024 * q:1024 * (q + 1)].unsqueeze(0)
                                ).then_inc(h_ow, 16)
                    s_ow.val += 16

                for hb in range(NH):
                    @block.tensor
                    def _(t, hb=hb):
                        t.wait_ge(h_ow, s_ow.val)
                        if hb > 0:
                            t.wait_ge(h_oe, s_oe.val)
                        for ch in range(2):
                            ins = t.matmul(pbc[:, 512 * ch:512 * (ch + 1)],
                                           ones16[:, :],
                                           csum16[0:1, hb, 512 * ch:512 * (ch + 1)],
                                           start=True, stop=True)
                        s_op.inc(ins)

                    @block.vector
                    def _(v, hb=hb):
                        v.wait_ge(h_op, s_op.val)
                        ins = v.tensor_tensor(VT[:, hb, :], VT[:, hb, :],
                                              pbc[:, :], ALU.mult)
                        s_oe.inc(ins)

                for tt in range(8):
                    @block.sync
                    def _(sy, tt=tt):
                        if tt == 0 and STAGE >= 5:
                            sy.wait_ge(h_fs, s_fs.val)
                        if tt > 0:
                            sy.wait_ge(h_oc, s_oc.val)
                        s_ox.inc(sy.dma_start(out=xow[:, :],
                                              in_=x_own[128 * tt:128 * (tt + 1), :]), 16)

                    for dc in range(4):
                        idx = 4 * tt + dc

                        @block.tensor
                        def _(t, tt=tt, dc=dc, idx=idx):
                            if idx == 0:
                                t.wait_ge(h_oe, s_oe.val)
                            if idx >= 2:
                                t.wait_ge(h_oc, idx - 1)
                            par = idx % 2
                            for hb in range(NH):
                                ins = t.matmul(po[:, 512 * par:512 * (par + 1)],
                                               VT[:, hb, 128 * tt:128 * (tt + 1)],
                                               wo16[:, hb, 512 * dc:512 * (dc + 1)],
                                               start=(hb == 0), stop=(hb == NH - 1))
                            s_op.inc(ins)

                        @block.vector
                        def _(v, tt=tt, dc=dc, idx=idx):
                            v.wait_ge(h_op, s_op.val)
                            v.wait_ge(h_ox, s_ox.val)
                            par = idx % 2
                            s_chn.inc(v.tensor_tensor(Pb[:, :], po[:, 512 * par:512 * (par + 1)],
                                                      g_sb[:, tt, 512 * dc:512 * (dc + 1)],
                                                      ALU.mult))
                            v.wait_ge(h_ch, s_chn.val)
                            ins = v.tensor_tensor(obuf[:, 512 * dc:512 * (dc + 1)],
                                                  Pb[:, :], xow[:, 512 * dc:512 * (dc + 1)],
                                                  ALU.add)
                            s_oc.inc(ins)

                    @block.sync
                    def _(sy, tt=tt):
                        sy.wait_ge(h_oc, s_oc.val)
                        s_ox.inc(sy.dma_start(out=out[128 * tt:128 * (tt + 1), :],
                                              in_=obuf[:, :]), 16)

                @block.sync
                def _(sy):
                    sy.wait_ge(h_ox, s_ox.val)

        if STAGE == 1:
            @block.gpsimd
            def _(g):
                g.wait_ge(h_sync, s_sync.val)
                s_st.inc(g.dma_start(out=dbg, in_=xnT_sh.ap()[0:TOK, 0:D]), 16)
        elif STAGE == 5:
            @block.gpsimd
            def _(g):
                g.wait_ge(h_fs, s_fs.val)
                s_st.inc(g.dma_start(
                    out=dbg.rearrange("(tt p) d -> p tt d", p=128),
                    in_=g_sb[:, :, :]), 16)
        elif STAGE == 4:
            @block.gpsimd
            def _(g):
                g.wait_ge(h_sync, s_sync.val)
                s_st.inc(g.dma_start(out=dbg[0:1024, 0:128], in_=wqgu_blk[0, 0:1024, :]), 16)
                s_st.inc(g.dma_start(out=dbg[0:1024, 128:256], in_=wqgu_blk[7, 9000:10024, :]), 16)
                s_st.inc(g.dma_start(out=dbg[0:1024, 256:384], in_=wqp_blk[0, 0:1024, :]), 16)
                s_st.inc(g.dma_start(out=dbg[0:1024, 384:512], in_=wqp_blk[33, 1024:2048, :]), 16)
        elif STAGE == 3:
            @block.gpsimd
            def _(g):
                g.wait_ge(h_sync, s_sync.val)
                s_st.inc(g.dma_start(
                    out=dbg[0:32, 0:2048],
                    in_=colsum_sh.ap().rearrange("h (a t) -> (h a) t", a=2)), 16)
        elif STAGE == 35:
            @block.gpsimd
            def _(g):
                g.wait_ge(h_cst, s_cst.val)
                s_st.inc(g.dma_start(out=dbg[0:128, 0:32], in_=racc[:, :]), 16)
                s_st.inc(g.dma_start(out=dbg[0:128, 32:40], in_=rsum[:, :]), 16)
                s_st.inc(g.dma_start(out=dbg[0:128, 40:48], in_=arec[:, :]), 16)
                s_st.inc(g.dma_start(out=dbg[0:128, 48:56], in_=a16[:, :]), 16)
                s_st.inc(g.dma_start(out=dbg[128:256, 0:2048], in_=Egrp[:, 0, 0:2048]), 16)
                s_st.inc(g.dma_start(out=dbg[256:384, 0:2048], in_=Egrp[:, 0, 2048:4096]), 16)
                s_st.inc(g.dma_start(out=dbg[384:386, 0:2048],
                                     in_=colsum_sh.ap()[0:1, :].rearrange("h (a t) -> (h a) t", a=2)), 16)
                s_st.inc(g.dma_start(out=dbg[512:640, 0:2048], in_=QT[:, 0:2048]), 16)
                s_st.inc(g.dma_start(out=dbg[640:768, 0:2048], in_=KT[:, 0:2048]), 16)
                s_st.inc(g.dma_start(out=dbg[386:387, 0:2048],
                                     in_=cget.ap()[0:1, 0:4, :].rearrange("a g t -> a (g t)")), 16)
                s_st.inc(g.dma_start(out=dbg[387:388, 0:2048],
                                     in_=cget.ap()[0:1, 4:8, :].rearrange("a g t -> a (g t)")), 16)
                s_st.inc(g.dma_start(out=dbg[768:896, 0:32],
                                     in_=aget.ap().rearrange("p g j -> p (g j)")), 16)
                s_st.inc(g.dma_start(out=dbg[896:1024, 0:2048], in_=eget[:, :]), 16)
                s_st.inc(g.dma_start(out=dbg[768:896, 32:40], in_=rget[:, :]), 16)
                s_st.inc(g.dma_start(out=dbg[768:896, 40:72], in_=r2get[:, :]), 16)
        elif STAGE == 2:
            @block.gpsimd
            def _(g):
                g.wait_ge(h_qe, s_qe.val)
                s_st.inc(g.dma_start(out=dbg[0:128, 0:2048], in_=QT[:, 0, 0:2048]), 16)
                s_st.inc(g.dma_start(out=dbg[128:256, 0:2048], in_=QT[:, 0, 2048:4096]), 16)
                s_st.inc(g.dma_start(out=dbg[256:384, 0:2048], in_=KT[:, 0, 0:2048]), 16)
                s_st.inc(g.dma_start(out=dbg[384:512, 0:2048], in_=KT[:, 0, 2048:4096]), 16)
                s_st.inc(g.dma_start(out=dbg[512:640, 0:1024], in_=VT[:, 0, :]), 16)
                s_st.inc(g.dma_start(out=dbg[640:768, 0:1024], in_=VT[:, 9, :]), 16)

            @block.gpsimd
            def _(g):
                g.wait_ge(h_st, s_st.val)

    return nc


_NC_CACHE = None


def _in_maps(X, Wqkv, Wo, Wgu, Wp):
    in_maps, meta = [], []
    for c in range(8):
        m = c % 2
        pair = c // 2
        b = pair % 2
        quad = {0: 0, 1: 1, 4: 2, 5: 3, 2: 0, 3: 1, 6: 2, 7: 3}[c]
        own = 1024 * quad
        hs = 128 * HPM * m
        w_qk_c = np.concatenate(
            [Wqkv[:, hs:hs + 1024], Wqkv[:, 2048 + hs:2048 + hs + 1024]], axis=1)
        idv = np.zeros((1, 16), np.int32)
        idv[0, 0] = c
        idv[0, 1] = m
        idv[0, 2] = quad
        in_maps.append({
            "x_half": np.ascontiguousarray(X[b, HALF * m:HALF * (m + 1), :]),
            "x_own": np.ascontiguousarray(X[b, own:own + TOK, :]),
            "w_qk": np.ascontiguousarray(w_qk_c),
            "w_v": np.ascontiguousarray(Wqkv[:, 4096:6144]),
            "w_o": Wo,
            "wgu_half": np.ascontiguousarray(Wgu[HF * m:HF * (m + 1), :]),
            "wp_half": np.ascontiguousarray(Wp[1024 * m:1024 * (m + 1), :]),
            "ids": idv,
        })
        meta.append((b, own))
    return in_maps, meta


def kernel(L0_attn_x, L0_attn_w_qkv, L0_attn_w_o, L0_ffn_w_gate_up, L0_ffn_w_proj):
    global _NC_CACHE
    X = np.ascontiguousarray(np.asarray(L0_attn_x, dtype=np.float32))
    Wqkv = np.ascontiguousarray(np.asarray(L0_attn_w_qkv, dtype=np.float32))
    Wo = np.ascontiguousarray(np.asarray(L0_attn_w_o, dtype=np.float32))
    Wgu = np.ascontiguousarray(np.asarray(L0_ffn_w_gate_up, dtype=np.float32))
    Wp = np.ascontiguousarray(np.asarray(L0_ffn_w_proj, dtype=np.float32))

    if _NC_CACHE is None:
        _NC_CACHE = build_program()
    nc = _NC_CACHE

    in_maps, meta = _in_maps(X, Wqkv, Wo, Wgu, Wp)
    res = run_bass_kernel_spmd(nc, in_maps, core_ids=list(range(8)))
    outp = np.zeros((B, S, D), np.float32)
    for c in range(8):
        b, own = meta[c]
        outp[b, own:own + TOK, :] = res.results[c]["out"]
    if STAGE < 99:
        kernel.debug = [r["dbg"] for r in res.results]
    return outp


# version 13 (final)
# speedup vs baseline: 1.6621x; 1.2277x over previous
import numpy as np
from contextlib import ExitStack
import concourse.bass as bass
import concourse.mybir as mybir
from concourse.bass_utils import run_bass_kernel_spmd

F32 = mybir.dt.float32
BF16 = mybir.dt.bfloat16
I32 = mybir.dt.int32
AF = mybir.ActivationFunctionType
ALU = mybir.AluOpType
AX = mybir.AxisListType

B = 2
S = 4096
D = 2048
NH = 16
DH = 128
HF = 8192
TOK = 1024
HALF = 2048
HPM = 8
INV_SQRT_DH = float(np.float32(1.0) / np.sqrt(np.float32(DH)))
RMS_EPS = float(np.finfo(np.float32).eps)
LN_EPS = 1e-5
MAGIC = 19_750_623

DT = D // 128
NQT = S // 128
GRP = 8
NGRP = NQT // GRP
NCH = HALF // 512

STAGE = 99


class Sem:

    def __init__(self, handle):
        self.h = handle
        self.val = 0

    def inc(self, ins, k=1):
        ins.then_inc(self.h, k)
        self.val += k
        return self.val


def _flag_row(sync_id, member):
    return 2 * sync_id + member


def build_program(detect_races=True):
    nc = bass.Bass("TRN2", target_bir_lowering=False, num_devices=8,
                   detect_race_conditions=detect_races)

    x_half = nc.dram_tensor("x_half", [HALF, D], F32, kind="ExternalInput").ap()
    x_own = nc.dram_tensor("x_own", [TOK, D], F32, kind="ExternalInput").ap()
    w_qk = nc.dram_tensor("w_qk", [D, 2048], F32, kind="ExternalInput").ap()
    w_v = nc.dram_tensor("w_v", [D, 2048], F32, kind="ExternalInput").ap()
    w_o = nc.dram_tensor("w_o", [D, D], F32, kind="ExternalInput").ap()
    wgu_half = nc.dram_tensor("wgu_half", [HF, D], F32, kind="ExternalInput").ap()
    wp_half = nc.dram_tensor("wp_half", [1024, HF], F32, kind="ExternalInput").ap()
    ids = nc.dram_tensor("ids", [1, 16], I32, kind="ExternalInput").ap()
    out = nc.dram_tensor("out", [TOK, D], F32, kind="ExternalOutput").ap()

    xb16 = nc.dram_tensor("xb16", [DT, HALF, 128], BF16)
    xnT_sh = nc.dram_tensor("xnT_sh", [D, S], BF16, addr_space="Shared")
    wqgu_blk = nc.dram_tensor("wqgu_blk", [DT, 2 * HF, 128], BF16, addr_space="Shared")
    wqp_blk = nc.dram_tensor("wqp_blk", [HF // 128, D, 128], BF16, addr_space="Shared")
    colsum_sh = nc.dram_tensor("colsum_sh", [NH, S], F32, addr_space="Shared")
    flags_sh = nc.dram_tensor("flags_sh", [16, 16], I32, addr_space="Shared")
    hT_loc = nc.dram_tensor("hT_loc", [HF, TOK], BF16)
    if STAGE < 99:
        dbg = nc.dram_tensor("dbg", [TOK, D], F32, kind="ExternalOutput").ap()

    es = ExitStack()
    with es:
        sb = lambda *a, **k: es.enter_context(nc.sbuf_tensor(*a, **k))
        sem = lambda n: es.enter_context(nc.semaphore(n))
        sids = sb("sids", [1, 16], I32)
        ones16 = sb("ones16", [1, 128], BF16)
        onescol = sb("onescol", [128, 1], BF16)
        ones32 = sb("ones32", [1, 128], F32)
        smagic = sb("smagic", [1, 16], I32)
        consts = sb("consts", [128, 4], F32)
        sflag = sb("sflag", [1, 32], I32)
        h_in = sem("s_in"); h_ln = sem("s_ln"); h_lnv = sem("s_lnv")
        h_mm = sem("s_mm"); h_st = sem("s_st"); h_poll = sem("s_poll")
        h_sync = sem("s_sync")
        h_ch = sem("s_ch")
        h_tp = sem("s_tp")
        h_xst = sem("s_xst")
        VT = es.enter_context(nc.sbuf_tensor("VT", [128, NH, TOK], BF16))
        g_sb = es.enter_context(nc.sbuf_tensor("g_sb", [128, TOK // 128, D], BF16))
        block = es.enter_context(nc.Block())
        s_in, s_ln, s_lnv, s_mm, s_st = Sem(h_in), Sem(h_ln), Sem(h_lnv), Sem(h_mm), Sem(h_st)
        s_sync = Sem(h_sync)
        s_chn = Sem(h_ch)
        s_tp = Sem(h_tp)
        s_xst = Sem(h_xst)

        esA = ExitStack()
        with esA:
            sbA = lambda *a, **k: esA.enter_context(nc.sbuf_tensor(*a, **k))
            psA = lambda *a, **k: esA.enter_context(nc.psum_tensor(*a, **k))
            xt = sbA("xt", [128, DT, 512], BF16, is_transpose_dest=True)
            xt2 = sbA("xt2", [128, DT, 512], BF16)
            rowm = sbA("rowm", [1, 512], F32)
            rows2 = sbA("rows2", [1, 512], F32)
            rowtmp = sbA("rowtmp", [1, 512], F32)
            rowvar = sbA("rowvar", [1, 512], F32)
            rowrstd = sbA("rowrstd", [1, 512], F32)
            rowm16 = sbA("rowm16", [1, 512], BF16)
            rowrstd16 = sbA("rowrstd16", [1, 512], BF16)
            mbc = sbA("mbc", [128, 512], BF16)
            rbc = sbA("rbc", [128, 512], BF16)
            cen = sbA("cen", [128, DT, 512], BF16)
            xnt_c = sbA("xnt_c", [128, DT, 512], BF16)
            pS = psA("pS", [1, 512], F32)
            pS2 = psA("pS2", [1, 512], F32)
            pM = psA("pM", [128, 512], F32)
            pR = psA("pR", [128, 512], F32)
            @block.gpsimd
            def _(g):
                g.memset(smagic[:, :], MAGIC)
                s_in.inc(g.dma_start(out=sids[:, :], in_=ids), 16)
                xr = x_half.rearrange("t (dt dd) -> dt t dd", dd=128)
                for dt in range(DT):
                    s_in.inc(g.dma_start(out=xb16[dt, :, :], in_=xr[dt, :, :]), 16)

            @block.vector
            def _(v):
                v.memset(ones16[:, :], 1.0)
                v.memset(ones32[:, :], 1.0)
                v.memset(consts[:, 0:1], LN_EPS)
                v.memset(consts[:, 1:2], RMS_EPS)
                s_lnv.inc(v.memset(onescol[:, :], 1.0))

            for c in range(NCH):
                @block.sync
                def _(sy, c=c):
                    if c == 0:
                        sy.wait_ge(h_in, 272)
                    else:
                        sy.wait_ge(h_xst, s_xst.val)
                    for dt in range(DT):
                        ins = sy.dma_start_transpose(
                            xt[:, dt, :], xb16[dt, 512 * c:512 * (c + 1), :])
                        s_tp.inc(ins, 16)

                @block.vector
                def _(v, c=c):
                    v.wait_ge(h_tp, s_tp.val)
                    s_lnv.inc(v.tensor_tensor(xt2[:, :, :], xt[:, :, :], xt[:, :, :], ALU.mult))

                @block.tensor
                def _(t, c=c):
                    t.wait_ge(h_lnv, s_lnv.val)
                    for dt in range(DT):
                        t.matmul(pS[:, :], onescol[:, :], xt[:, dt, :],
                                 start=(dt == 0), stop=(dt == DT - 1))
                    for dt in range(DT):
                        ins = t.matmul(pS2[:, :], onescol[:, :], xt2[:, dt, :],
                                       start=(dt == 0), stop=(dt == DT - 1))
                    s_mm.inc(ins)

                @block.scalar
                def _(s, c=c):
                    s.wait_ge(h_mm, s_mm.val)
                    s.activation(rowm[:, :], pS[:, :], AF.Copy, scale=1.0 / D)
                    s_ln.inc(s.activation(rows2[:, :], pS2[:, :], AF.Copy, scale=1.0 / D))

                @block.vector
                def _(v, c=c):
                    v.wait_ge(h_ln, s_ln.val)
                    s_chn.inc(v.tensor_tensor(rowtmp[:, :], rowm[:, :], rowm[:, :], ALU.mult))
                    v.wait_ge(h_ch, s_chn.val)
                    s_lnv.inc(v.tensor_tensor(rowvar[:, :], rows2[:, :], rowtmp[:, :], ALU.subtract))

                @block.scalar
                def _(s, c=c):
                    s.wait_ge(h_lnv, s_lnv.val)
                    s_chn.inc(s.activation(rowtmp[:, :], rowvar[:, :], AF.Ln, bias=consts[0:1, 0:1]))
                    s.wait_ge(h_ch, s_chn.val)
                    s_ln.inc(s.activation(rowrstd[:, :], rowtmp[:, :], AF.Exp, scale=-0.5))

                @block.vector
                def _(v, c=c):
                    v.wait_ge(h_ln, s_ln.val)
                    v.tensor_copy(rowm16[:, :], rowm[:, :])
                    s_lnv.inc(v.tensor_copy(rowrstd16[:, :], rowrstd[:, :]))

                @block.tensor
                def _(t, c=c):
                    t.wait_ge(h_lnv, s_lnv.val)
                    t.matmul(pM[:, :], ones16[:, :], rowm16[:, :], start=True, stop=True)
                    s_mm.inc(t.matmul(pR[:, :], ones16[:, :], rowrstd16[:, :], start=True, stop=True))

                @block.vector
                def _(v, c=c):
                    v.wait_ge(h_mm, s_mm.val)
                    v.tensor_copy(mbc[:, :], pM[:, :])
                    s_chn.inc(v.tensor_copy(rbc[:, :], pR[:, :]))
                    v.wait_ge(h_ch, s_chn.val)
                    s_chn.inc(v.tensor_tensor(cen[:, :, :], xt[:, :, :], 
                                              mbc[:, :].unsqueeze(1).broadcast_to([128, DT, 512]), ALU.subtract))
                    v.wait_ge(h_ch, s_chn.val)
                    s_lnv.inc(v.tensor_tensor(xnt_c[:, :, :], cen[:, :, :],
                                              rbc[:, :].unsqueeze(1).broadcast_to([128, DT, 512]), ALU.mult))

                @block.gpsimd
                def _(g, c=c):
                    g.wait_ge(h_lnv, s_lnv.val)
                    with g.register(f"mem{c}") as mem:
                        g.reg_load(mem, sids[0:1, 1:2])
                        with g.If_eq(mem, 0):
                            g.dma_start(
                                out=xnT_sh.ap()[:, 512 * c:512 * (c + 1)].rearrange(
                                    "(dt p) t -> p dt t", p=128),
                                in_=xnt_c[:, :, :]).then_inc(h_xst, 16)
                        with g.Else():
                            g.dma_start(
                                out=xnT_sh.ap()[:, HALF + 512 * c:HALF + 512 * (c + 1)].rearrange(
                                    "(dt p) t -> p dt t", p=128),
                                in_=xnt_c[:, :, :]).then_inc(h_xst, 16)
                    s_xst.val += 16

            @block.gpsimd
            def _(g):
                g.wait_ge(h_xst, s_xst.val)
                with g.register("memS0") as mem:
                    g.reg_load(mem, sids[0:1, 1:2])
                    with g.If_eq(mem, 0):
                        g.dma_start(out=flags_sh[0:1, :], in_=smagic[:, :]).then_inc(h_poll, 16)
                    with g.Else():
                        g.dma_start(out=flags_sh[1:2, :], in_=smagic[:, :]).then_inc(h_poll, 16)
                g.wait_ge(h_poll, 16)
                with g.register("ndS0") as nd, g.register("tgtS0") as tgt:
                    g.reg_mov(nd, 1)
                    g.reg_mov(tgt, 16)
                    with g.While(nd):
                        g.reg_alu(tgt, tgt, 16, ALU.add)
                        g.dma_start(out=sflag[:, 0:32],
                                    in_=flags_sh.ap()[0:2, :].rearrange("a b -> (a b)").unsqueeze(0)
                                    ).then_inc(h_poll, 16)
                        g.wait_ge(h_poll, tgt)
                        with g.register("fv0S0") as fv0, g.register("fv1S0") as fv1:
                            g.reg_load(fv0, sflag[0:1, 0:1])
                            g.reg_load(fv1, sflag[0:1, 16:17])
                            g.reg_alu(fv0, fv0, MAGIC, ALU.not_equal)
                            g.reg_alu(fv1, fv1, MAGIC, ALU.not_equal)
                            g.reg_alu(nd, fv0, fv1, ALU.logical_or)
                s_sync.inc(g.nop())

        esB = ExitStack()
        with esB:
            sbB = lambda *a, **k: esB.enter_context(nc.sbuf_tensor(*a, **k))
            psB = lambda *a, **k: esB.enter_context(nc.psum_tensor(*a, **k))
            xnc2 = [sbB(f"xnc{i}", [128, DT, 512], BF16) for i in range(2)]
            xnc = xnc2[0]
            whead = sbB("whead", [128, DT, 256], BF16)
            QT = sbB("QT", [128, S], BF16)
            KT = sbB("KT", [128, S], BF16)
            racc = sbB("racc", [128, GRP * 4], F32)
            rsum = sbB("rsum", [128, GRP], F32)
            arec = sbB("arec", [128, GRP], F32)
            a16 = sbB("a16", [128, GRP], BF16)
            cstage = sbB("cstage", [1, S], F32)
            if STAGE == 35:
                cget = sbB("cget", [1, 8, 512], F32)
                aget = sbB("aget", [128, 4, GRP], F32)
                eget = sbB("eget", [128, 2048], BF16)
                rget = sbB("rget", [128, GRP], F32)
                r2get = sbB("r2get", [128, GRP * 4], F32)
            pBig = [psB(f"pBig{i}", [128, 1024], F32) for i in range(4)]
            pQK = [pBig[i][:, 0:512] for i in range(4)]
            h_wl = es.enter_context(nc.semaphore("s_wl"))
            h_xc = es.enter_context(nc.semaphore("s_xc"))
            h_xcg = es.enter_context(nc.semaphore("s_xcg"))
            h_qp = es.enter_context(nc.semaphore("s_qp"))
            h_qe = es.enter_context(nc.semaphore("s_qe"))
            h_sc = es.enter_context(nc.semaphore("s_sc"))
            h_ex = es.enter_context(nc.semaphore("s_ex"))
            h_cse = es.enter_context(nc.semaphore("s_cse"))
            h_csb = es.enter_context(nc.semaphore("s_csb"))
            h_csv = es.enter_context(nc.semaphore("s_csv"))
            h_cst = es.enter_context(nc.semaphore("s_cst"))
            s_wl, s_xc, s_qp, s_qe = Sem(h_wl), Sem(h_xc), Sem(h_qp), Sem(h_qe)
            s_xcg = Sem(h_xcg)
            s_sc, s_ex, s_cse = Sem(h_sc), Sem(h_ex), Sem(h_cse)
            s_csb, s_csv, s_cst = Sem(h_csb), Sem(h_csv), Sem(h_cst)

            if STAGE == 35:
                @block.vector
                def _(v):
                    v.memset(cget[:, :, :], 0.0)
                    v.memset(eget[:, :], 0.0)
                    v.memset(rget[:, :], 0.0)
                    v.memset(r2get[:, :], 0.0)
                    v.memset(aget[:, :, :], 0.0)

            def emit_attention_head(hl, Egrp):
                @block.vector
                def _(v):
                    if s_cst.val > 0:
                        v.wait_ge(h_cst, s_cst.val)
                    v.memset(cstage[:, :], 0.0)

                for g_i in range(NGRP):
                    @block.tensor
                    def _(t, g_i=g_i):
                        if g_i == 0:
                            t.wait_ge(h_qe, s_qe.val)
                        for j in range(GRP):
                            i = g_i * GRP + j
                            for qtr in range(4):
                                q_idx = s_sc.val
                                if q_idx >= 2:
                                    t.wait_ge(h_ex, q_idx - 1)
                                par = q_idx % 2
                                for kc in range(2):
                                    ins = t.matmul(
                                        pBig[par][:, 512 * kc:512 * (kc + 1)],
                                        QT[:, 128 * i:128 * (i + 1)],
                                        KT[:, 1024 * qtr + 512 * kc:1024 * qtr + 512 * (kc + 1)],
                                        start=True, stop=True)
                                s_sc.inc(ins)

                    @block.scalar
                    def _(s, g_i=g_i):
                        base0 = s_sc.val - 32
                        for j in range(GRP):
                            for qtr in range(4):
                                base = base0 + 4 * j + qtr + 1
                                s.wait_ge(h_sc, base)
                                if j == 0 and qtr == 0 and s_cse.val > 0:
                                    s.wait_ge(h_cse, s_cse.val)
                                par = (base - 1) % 2
                                ins = s.activation(
                                    Egrp[:, j, 1024 * qtr:1024 * (qtr + 1)],
                                    pBig[par][:, :], AF.Exp, scale=INV_SQRT_DH,
                                    accum_out=racc[:, 4 * j + qtr:4 * j + qtr + 1])
                                s_ex.inc(ins)

                    @block.vector
                    def _(v, g_i=g_i):
                        v.wait_ge(h_ex, s_ex.val)
                        s_chn.inc(v.tensor_reduce(rsum[:, :],
                                        racc[:, :].rearrange("p (g q) -> p g q", q=4),
                                        axis=AX.X, op=ALU.add))
                        v.wait_ge(h_ch, s_chn.val)
                        s_chn.inc(v.reciprocal(arec[:, :], rsum[:, :]))
                        v.wait_ge(h_ch, s_chn.val)
                        s_cse.inc(v.tensor_copy(a16[:, :], arec[:, :]))

                    @block.tensor
                    def _(t, g_i=g_i):
                        t.wait_ge(h_cse, s_cse.val)
                        for kc in range(8):
                            if s_csb.val >= 2:
                                t.wait_ge(h_csv, s_csb.val - 1)
                            pcs = pBig[2][0:1, 512 * (kc % 2):512 * (kc % 2 + 1)]
                            for j in range(GRP):
                                ins = t.matmul(pcs, a16[:, j:j + 1],
                                               Egrp[:, j, 512 * kc:512 * (kc + 1)],
                                               start=(j == 0), stop=(j == GRP - 1))
                            s_csb.inc(ins)

                    @block.vector
                    def _(v, g_i=g_i):
                        base0 = s_csb.val - 8
                        for kc in range(8):
                            v.wait_ge(h_csb, base0 + kc + 1)
                            pcs = pBig[2][0:1, 512 * (kc % 2):512 * (kc % 2 + 1)]
                            ins = v.scalar_tensor_tensor(
                                cstage[0:1, 512 * kc:512 * (kc + 1)],
                                pcs, 1.0,
                                cstage[0:1, 512 * kc:512 * (kc + 1)],
                                ALU.mult, ALU.add)
                            s_csv.inc(ins)

                @block.gpsimd
                def _(g, hl=hl):
                    g.wait_ge(h_csv, s_csv.val)
                    with g.register(f"memH{hl}") as mem:
                        g.reg_load(mem, sids[0:1, 1:2])
                        with g.If_eq(mem, 0):
                            g.dma_start(out=colsum_sh[hl:hl + 1, :],
                                        in_=cstage[:, :]).then_inc(h_cst, 16)
                        with g.Else():
                            g.dma_start(out=colsum_sh[HPM + hl:HPM + hl + 1, :],
                                        in_=cstage[:, :]).then_inc(h_cst, 16)
                    s_cst.val += 16

            esC = ExitStack()
            with esC:
                Egrp = esC.enter_context(nc.sbuf_tensor("Egrp", [128, GRP, S], BF16))
                NB = 1 if STAGE in (2, 35) else HPM
                for hl in range(NB):
                    @block.gpsimd
                    def _(g, hl=hl):
                        if hl > 0:
                            g.wait_ge(h_qe, s_qe.val)
                        for qk in range(2):
                            cs = 1024 * qk + 128 * hl
                            ins = g.dma_start(
                                out=whead[:, :, 128 * qk:128 * (qk + 1)],
                                in_=w_qk[:, cs:cs + 128].rearrange(
                                    "(dt p) c -> p dt c", p=128))
                            s_wl.inc(ins, 16)

                    qp_base = s_qp.val
                    qe_base = s_qe.val

                    @block.sync
                    def _(sy, hl=hl, qp_base=qp_base):
                        sy.wait_ge(h_sync, s_sync.val)
                        for c in range(S // 512):
                            if hl > 0 or c >= 2:
                                sy.wait_ge(h_qp, qp_base + max(0, c - 1))
                            s_xc.inc(sy.dma_start(
                                out=xnc2[c % 2][:, :, :],
                                in_=xnT_sh.ap()[:, 512 * c:512 * (c + 1)].rearrange(
                                    "(dt p) t -> p dt t", p=128)), 16)
                    sxc_base = s_xc.val - 16 * 8

                    @block.tensor
                    def _(t, hl=hl, sxc_base=sxc_base, qe_base=qe_base):
                        t.wait_ge(h_wl, s_wl.val)
                        if s_csv.val > 0:
                            t.wait_ge(h_csv, s_csv.val)
                        for c in range(S // 512):
                            t.wait_ge(h_xc, sxc_base + 16 * (c + 1))
                            if c >= 2:
                                t.wait_ge(h_qe, qe_base + c - 1)
                            for qk in range(2):
                                for dt in range(DT):
                                    ins = t.matmul(
                                        pQK[2 * (c % 2) + qk],
                                        whead[:, dt, 128 * qk:128 * (qk + 1)],
                                        xnc2[c % 2][:, dt, :],
                                        start=(dt == 0), stop=(dt == DT - 1))
                                if qk == 1:
                                    s_qp.inc(ins)

                    @block.vector
                    def _(v, hl=hl):
                        base0 = s_qp.val - 8
                        for c in range(S // 512):
                            v.wait_ge(h_qp, base0 + c + 1)
                            v.tensor_copy(QT[:, 512 * c:512 * (c + 1)], pQK[2 * (c % 2)])
                            ins = v.tensor_copy(KT[:, 512 * c:512 * (c + 1)], pQK[2 * (c % 2) + 1])
                            s_qe.inc(ins)

                    if STAGE >= 3:
                        emit_attention_head(hl, Egrp)

            esD = ExitStack()
            with esD:
             if STAGE != 35:
                  wv16 = esD.enter_context(nc.sbuf_tensor("wv16", [128, DT, 2048], BF16))

                  @block.gpsimd
                  def _(g):
                      g.wait_ge(h_qe, s_qe.val)
                      if STAGE >= 3:
                          g.wait_ge(h_cst, s_cst.val)
                      s_wl.inc(g.dma_start(
                          out=wv16[:, :, :],
                          in_=w_v.rearrange("(dt p) c -> p dt c", p=128)), 16)

                  for oc in range(TOK // 512):
                      @block.gpsimd
                      def _(g, oc=oc):
                          g.wait_ge(h_wl, s_wl.val)
                          if oc > 0:
                              g.wait_ge(h_qe, s_qe.val)
                          with g.register(f"quadV{oc}") as qd:
                              g.reg_load(qd, sids[0:1, 2:3])
                              for q in range(4):
                                  with g.If_eq(qd, q):
                                      cs = 1024 * q + 512 * oc
                                      g.dma_start(
                                          out=xnc[:, :, :],
                                          in_=xnT_sh.ap()[:, cs:cs + 512].rearrange(
                                              "(dt p) t -> p dt t", p=128)).then_inc(h_xcg, 16)
                          s_xcg.val += 16

                      for grp in range(4):
                          @block.tensor
                          def _(t, oc=oc, grp=grp):
                              t.wait_ge(h_xcg, s_xcg.val)
                              if oc == 0 and grp == 0 and s_csv.val > 0:
                                  t.wait_ge(h_csv, s_csv.val)
                              if oc > 0 or grp > 0:
                                  t.wait_ge(h_qe, s_qe.val)
                              for h2 in range(4):
                                  hh = 4 * grp + h2
                                  for dt in range(DT):
                                      ins = t.matmul(pQK[h2],
                                                     wv16[:, dt, 128 * hh:128 * (hh + 1)],
                                                     xnc[:, dt, :],
                                                     start=(dt == 0), stop=(dt == DT - 1))
                              s_qp.inc(ins)

                          @block.vector
                          def _(v, oc=oc, grp=grp):
                              v.wait_ge(h_qp, s_qp.val)
                              for h2 in range(4):
                                  hh = 4 * grp + h2
                                  ins = v.tensor_copy(VT[:, hh, 512 * oc:512 * (oc + 1)],
                                                      pQK[h2])
                              s_qe.inc(ins)

        if STAGE >= 3:
            h_poll1 = es.enter_context(nc.semaphore("s_poll1"))

            @block.gpsimd
            def _(g):
                g.wait_ge(h_cst, s_cst.val)
                with g.register("memC") as mem:
                    g.reg_load(mem, sids[0:1, 1:2])
                    with g.If_eq(mem, 0):
                        g.dma_start(out=flags_sh[2:3, :], in_=smagic[:, :]).then_inc(h_poll1, 16)
                    with g.Else():
                        g.dma_start(out=flags_sh[3:4, :], in_=smagic[:, :]).then_inc(h_poll1, 16)
                g.wait_ge(h_poll1, 16)
                with g.register("ndC") as nd, g.register("tgtC") as tgt:
                    g.reg_mov(nd, 1)
                    g.reg_mov(tgt, 16)
                    with g.While(nd):
                        g.reg_alu(tgt, tgt, 16, ALU.add)
                        g.dma_start(out=sflag[:, 0:32],
                                    in_=flags_sh.ap()[2:4, :].rearrange("a b -> (a b)").unsqueeze(0)
                                    ).then_inc(h_poll1, 16)
                        g.wait_ge(h_poll1, tgt)
                        with g.register("fv0C") as fv0, g.register("fv1C") as fv1:
                            g.reg_load(fv0, sflag[0:1, 0:1])
                            g.reg_load(fv1, sflag[0:1, 16:17])
                            g.reg_alu(fv0, fv0, MAGIC, ALU.not_equal)
                            g.reg_alu(fv1, fv1, MAGIC, ALU.not_equal)
                            g.reg_alu(nd, fv0, fv1, ALU.logical_or)
                s_sync.inc(g.nop())

        if STAGE >= 4:
            h_qw = es.enter_context(nc.semaphore("s_qw"))
            h_qs = es.enter_context(nc.semaphore("s_qs"))
            h_qv = es.enter_context(nc.semaphore("s_qv"))
            s_qw, s_qs, s_qv = Sem(h_qw), Sem(h_qs), Sem(h_qv)
            esQ = ExitStack()
            with esQ:
                sbQ = lambda *a, **k: esQ.enter_context(nc.sbuf_tensor(*a, **k))
                QW = 2048
                G = QW // 64
                qD = [{
                    "w16": sbQ(f"w16_{i}", [128, QW], BF16),
                    "t16": sbQ(f"t16_{i}", [128, QW], BF16),
                    "tmp1": sbQ(f"tmp1_{i}", [128, QW], BF16),
                    "tmp2": sbQ(f"tmp2_{i}", [128, QW], BF16),
                    "wqb": sbQ(f"wqb_{i}", [128, QW], BF16),
                    "gsum": sbQ(f"gsum_{i}", [128, G], F32),
                    "scl16": sbQ(f"scl16_{i}", [128, G], BF16),
                    "rcp": sbQ(f"rcp_{i}", [128, G], F32),
                } for i in range(2)]

                qtiles = ([("gu", r, 0) for r in range(64)]
                          + [("p", r, c) for r in range(8) for c in range(4)])
                qw_targets = {}

                def emit_qload(g, ti):
                    kind, r, cc = qtiles[ti]
                    qw16 = qD[ti % 2]["w16"]
                    if kind == "gu":
                        ins = g.dma_start(out=qw16[:, :],
                                          in_=wgu_half[128 * r:128 * (r + 1), :])
                    else:
                        ins = g.dma_start(out=qw16[:, :],
                                          in_=wp_half[128 * r:128 * (r + 1),
                                                      QW * cc:QW * (cc + 1)])
                    s_qw.inc(ins, 16)
                    qw_targets[ti] = s_qw.val

                @block.gpsimd
                def _(g):
                    g.wait_ge(h_qe, s_qe.val)
                    g.wait_ge(h_cst, s_cst.val)
                    emit_qload(g, 0)
                    emit_qload(g, 1)

                for ti, (kind, r, cc) in enumerate(qtiles):
                    B = qD[ti % 2]
                    w16, t16, tmp1, tmp2 = B["w16"], B["t16"], B["tmp1"], B["tmp2"]
                    wqb, gsum, scl16, rcp = B["wqb"], B["gsum"], B["scl16"], B["rcp"]

                    @block.vector
                    def _(v, ti=ti):
                        v.wait_ge(h_qw, qw_targets[ti])
                        w3 = w16[:, :].rearrange("p (g s) -> p g s", s=64)
                        s_chn.inc(v.tensor_reduce(gsum[:, :], w3, axis=AX.X,
                                                  op=ALU.add, apply_absolute_value=True))
                        v.wait_ge(h_ch, s_chn.val)
                        s_chn.inc(v.tensor_scalar(scl16[:, :], gsum[:, :],
                                                  1.0 / 64, 1e-8, ALU.mult, ALU.max))
                        v.wait_ge(h_ch, s_chn.val)
                        s_chn.inc(v.reciprocal(rcp[:, :], scl16[:, :]))
                        v.wait_ge(h_ch, s_chn.val)
                        rb = rcp[:, :].unsqueeze(2).broadcast_to([128, G, 64])
                        s_chn.inc(v.tensor_tensor(
                            t16[:, :].rearrange("p (g s) -> p g s", s=64),
                            w3, rb, ALU.mult))
                        v.wait_ge(h_ch, s_chn.val)
                        s_chn.inc(v.tensor_scalar(tmp1[:, :], t16[:, :], 0.5, None, ALU.is_gt))
                        s_chn.inc(v.tensor_scalar(tmp2[:, :], t16[:, :], -0.5, None, ALU.is_lt))
                        v.wait_ge(h_ch, s_chn.val)
                        s_chn.inc(v.tensor_tensor(t16[:, :], tmp1[:, :], tmp2[:, :],
                                                  ALU.subtract))
                        v.wait_ge(h_ch, s_chn.val)
                        sb = scl16[:, :].unsqueeze(2).broadcast_to([128, G, 64])
                        s_chn.inc(v.tensor_tensor(
                            tmp1[:, :].rearrange("p (g s) -> p g s", s=64),
                            t16[:, :].rearrange("p (g s) -> p g s", s=64), sb, ALU.mult))
                        v.wait_ge(h_ch, s_chn.val)
                        s_chn.inc(v.tensor_tensor(tmp2[:, :], tmp1[:, :], w16[:, :],
                                                  ALU.subtract))
                        v.wait_ge(h_ch, s_chn.val)
                        if ti > 1:
                            v.wait_ge(h_qs, 16 * (ti - 1))
                        s_chn.inc(v.tensor_tensor(wqb[:, :], w16[:, :], tmp2[:, :],
                                                  ALU.add))
                        v.wait_ge(h_ch, s_chn.val)
                        s_qv.inc(v.nop())

                    @block.gpsimd
                    def _(g, ti=ti, kind=kind, r=r, cc=cc):
                        g.wait_ge(h_qv, ti + 1)
                        with g.register(f"memQ{ti}") as mem:
                            g.reg_load(mem, sids[0:1, 1:2])
                            for mm_ in range(2):
                                if kind == "gu":
                                    dst0 = HF * mm_ + 128 * r
                                    blk_slice = wqgu_blk.ap()[:, dst0:dst0 + 128, :]
                                else:
                                    dst0 = 1024 * mm_ + 128 * r
                                    blk_slice = wqp_blk.ap()[16 * cc:16 * (cc + 1),
                                                             dst0:dst0 + 128, :]
                                with (g.If_eq(mem, 0) if mm_ == 0 else g.Else()):
                                    g.dma_start(
                                        out=blk_slice.rearrange("dt p dd -> p dt dd"),
                                        in_=wqb[:, :].rearrange("p (dt dd) -> p dt dd", dd=128)
                                    ).then_inc(h_qs, 16)
                        s_qs.val += 16
                        if ti + 2 < len(qtiles):
                            emit_qload(g, ti + 2)

            h_poll2 = es.enter_context(nc.semaphore("s_poll2"))

            @block.gpsimd
            def _(g):
                g.wait_ge(h_qs, s_qs.val)
                with g.register("memW") as mem:
                    g.reg_load(mem, sids[0:1, 1:2])
                    with g.If_eq(mem, 0):
                        g.dma_start(out=flags_sh[4:5, :], in_=smagic[:, :]).then_inc(h_poll2, 16)
                    with g.Else():
                        g.dma_start(out=flags_sh[5:6, :], in_=smagic[:, :]).then_inc(h_poll2, 16)
                g.wait_ge(h_poll2, 16)
                with g.register("ndW") as nd, g.register("tgtW") as tgt:
                    g.reg_mov(nd, 1)
                    g.reg_mov(tgt, 16)
                    with g.While(nd):
                        g.reg_alu(tgt, tgt, 16, ALU.add)
                        g.dma_start(out=sflag[:, 0:32],
                                    in_=flags_sh.ap()[4:6, :].rearrange("a b -> (a b)").unsqueeze(0)
                                    ).then_inc(h_poll2, 16)
                        g.wait_ge(h_poll2, tgt)
                        with g.register("fv0W") as fv0, g.register("fv1W") as fv1:
                            g.reg_load(fv0, sflag[0:1, 0:1])
                            g.reg_load(fv1, sflag[0:1, 16:17])
                            g.reg_alu(fv0, fv0, MAGIC, ALU.not_equal)
                            g.reg_alu(fv1, fv1, MAGIC, ALU.not_equal)
                            g.reg_alu(nd, fv0, fv1, ALU.logical_or)
                s_sync.inc(g.nop())

        if STAGE >= 5:
            h_fw = es.enter_context(nc.semaphore("s_fw"))
            h_fx = es.enter_context(nc.semaphore("s_fx"))
            h_fp = es.enter_context(nc.semaphore("s_fp"))
            h_fe = es.enter_context(nc.semaphore("s_fe"))
            h_fh = es.enter_context(nc.semaphore("s_fh"))
            h_fs = es.enter_context(nc.semaphore("s_fs"))
            s_fw, s_fx, s_fp, s_fe = Sem(h_fw), Sem(h_fx), Sem(h_fp), Sem(h_fe)
            s_fh, s_fs = Sem(h_fh), Sem(h_fs)

            esF = ExitStack()
            with esF:
                sbF = lambda *a, **k: esF.enter_context(nc.sbuf_tensor(*a, **k))
                psF = lambda *a, **k: esF.enter_context(nc.psum_tensor(*a, **k))
                inv_col = sbF("inv_col", [128, 8], F32)
                pcolF = psF("pcolF", [128, 8], F32)

                h_fpg = es.enter_context(nc.semaphore("s_fpg"))
                h_fpu = es.enter_context(nc.semaphore("s_fpu"))
                h_feu = es.enter_context(nc.semaphore("s_feu"))
                h_f2 = es.enter_context(nc.semaphore("s_f2"))
                h_fpz = es.enter_context(nc.semaphore("s_fpz"))
                s_fpg, s_fpu, s_feu, s_f2 = Sem(h_fpg), Sem(h_fpu), Sem(h_feu), Sem(h_f2)
                s_fpz = Sem(h_fpz)

                esG = ExitStack()
                with esG:
                    xno = esG.enter_context(nc.sbuf_tensor("xno", [128, DT, TOK], BF16))
                    invrow = esG.enter_context(nc.sbuf_tensor("invrow", [1, TOK], F32))
                    hsrow = esG.enter_context(nc.sbuf_tensor("hsrow", [1, TOK], F32))

                    @block.gpsimd
                    def _(g):
                        g.wait_ge(h_sync, s_sync.val)
                        with g.register("quadF") as qd:
                            g.reg_load(qd, sids[0:1, 2:3])
                            for q in range(4):
                                with g.If_eq(qd, q):
                                    g.dma_start(
                                        out=xno[:, :, :],
                                        in_=xnT_sh.ap()[:, 1024 * q:1024 * (q + 1)].rearrange(
                                            "(dt p) t -> p dt t", p=128)).then_inc(h_fx, 16)
                        s_fx.val += 16

                    gat = [esG.enter_context(nc.sbuf_tensor(f"gat{i}", [128, DT, 512], BF16)) for i in range(2)]
                    upt = [esG.enter_context(nc.sbuf_tensor(f"upt{i}", [128, DT, 512], BF16)) for i in range(2)]
                    sgb = esG.enter_context(nc.sbuf_tensor("sgb", [128, 1024], BF16))
                    upe = esG.enter_context(nc.sbuf_tensor("upe", [128, 1024], BF16))
                    hTt = [esG.enter_context(nc.sbuf_tensor(f"hTt{i}", [128, 1024], BF16)) for i in range(2)]
                    h2t = esG.enter_context(nc.sbuf_tensor("h2t", [128, 1024], BF16))
                    h2row = esG.enter_context(nc.sbuf_tensor("h2row", [1, TOK], F32))
                    pg = esG.enter_context(nc.psum_tensor("pg", [128, 1024], F32))
                    pu = esG.enter_context(nc.psum_tensor("pu", [128, 1024], F32))
                    ph2 = esG.enter_context(nc.psum_tensor("ph2", [1, 1024], F32))

                    @block.vector
                    def _(v):
                        v.memset(h2row[:, :], 0.0)

                    NGF = 16
                    fpg0, fpu0, fs0, fe0, feu0, fh0, f20 = (s_fpg.val, s_fpu.val,
                        s_fs.val, s_fe.val, s_feu.val, s_fh.val, s_f2.val)

                    for hg in range(NGF):
                        @block.sync
                        def _(sy, hg=hg):
                            if hg == 0:
                                sy.wait_ge(h_sync, s_sync.val)
                            if hg >= 2:
                                sy.wait_ge(h_fpu, fpu0 + 4 * (hg - 1))
                            for dt in range(DT):
                                sy.dma_start_transpose(
                                    gat[hg % 2][:, dt, :],
                                    wqgu_blk[dt, 512 * hg:512 * (hg + 1), :]
                                ).then_inc(h_fw, 16)
                                sy.dma_start_transpose(
                                    upt[hg % 2][:, dt, :],
                                    wqgu_blk[dt, HF + 512 * hg:HF + 512 * (hg + 1), :]
                                ).then_inc(h_fw, 16)
                            s_fw.val += 32 * DT

                        def emit_h2_mm(t, ht):
                            htl = ht % 4
                            hgl = ht // 4
                            t.wait_ge(h_fe, fe0 + ht + 1)
                            if htl == 0 and hgl >= 1:
                                t.wait_ge(h_fpz, hgl)
                            for ch in range(2):
                                ins = t.matmul(ph2[0:1, 512 * ch:512 * (ch + 1)],
                                               onescol[:, :],
                                               h2t[:, 512 * ch:512 * (ch + 1)],
                                               start=(htl == 0), stop=(htl == 3))
                            s_f2.inc(ins)

                        @block.tensor
                        def _(t, hg=hg):
                            t.wait_ge(h_fw, s_fw.val)
                            if hg == 0:
                                t.wait_ge(h_fx, s_fx.val)
                            for htl in range(4):
                                ht = 4 * hg + htl
                                if ht > 0:
                                    t.wait_ge(h_fs, fs0 + ht)
                                for ch in range(2):
                                    for dt in range(DT):
                                        ins = t.matmul(pg[:, 512 * ch:512 * (ch + 1)],
                                                 gat[hg % 2][:, dt, 128 * htl:128 * (htl + 1)],
                                                 xno[:, dt, 512 * ch:512 * (ch + 1)],
                                                 start=(dt == 0), stop=(dt == DT - 1))
                                s_fpg.inc(ins)
                                if ht > 0:
                                    t.wait_ge(h_feu, feu0 + ht)
                                for ch in range(2):
                                    for dt in range(DT):
                                        ins = t.matmul(pu[:, 512 * ch:512 * (ch + 1)],
                                                       upt[hg % 2][:, dt, 128 * htl:128 * (htl + 1)],
                                                       xno[:, dt, 512 * ch:512 * (ch + 1)],
                                                       start=(dt == 0), stop=(dt == DT - 1))
                                s_fpu.inc(ins)
                                if ht >= 1:
                                    emit_h2_mm(t, ht - 1)
                                if ht == 63:
                                    emit_h2_mm(t, 63)

                        @block.scalar
                        def _(s, hg=hg):
                            for htl in range(4):
                                ht = 4 * hg + htl
                                s.wait_ge(h_fpg, fpg0 + ht + 1)
                                if ht >= 1:
                                    s.wait_ge(h_fe, fe0 + ht)
                                ins = s.activation(sgb[:, :], pg[:, :], AF.Silu)
                                s_fs.inc(ins)

                        @block.vector
                        def _(v, hg=hg):
                            for htl in range(4):
                                ht = 4 * hg + htl
                                v.wait_ge(h_fpu, fpu0 + ht + 1)
                                if ht >= 1:
                                    v.wait_ge(h_fe, fe0 + ht)
                                s_feu.inc(v.tensor_copy(upe[:, :], pu[:, :]))
                                v.wait_ge(h_fs, fs0 + ht + 1)
                                v.wait_ge(h_feu, s_feu.val)
                                if ht >= 2:
                                    v.wait_ge(h_fh, fh0 + 16 * (ht - 1))
                                s_chn.inc(v.tensor_tensor(hTt[ht % 2][:, :], sgb[:, :],
                                                          upe[:, :], ALU.mult))
                                v.wait_ge(h_ch, s_chn.val)
                                if ht >= 1:
                                    v.wait_ge(h_f2, f20 + ht)
                                s_chn.inc(v.tensor_tensor(h2t[:, :], hTt[ht % 2][:, :],
                                                          hTt[ht % 2][:, :], ALU.mult))
                                v.wait_ge(h_ch, s_chn.val)
                                s_fe.inc(v.nop())

                        @block.vector
                        def _(v, hg=hg):
                            v.wait_ge(h_f2, f20 + 4 * (hg + 1))
                            for ch in range(2):
                                ins = v.scalar_tensor_tensor(
                                    h2row[0:1, 512 * ch:512 * (ch + 1)],
                                    ph2[0:1, 512 * ch:512 * (ch + 1)], 1.0,
                                    h2row[0:1, 512 * ch:512 * (ch + 1)],
                                    ALU.mult, ALU.add)
                            s_chn.inc(ins)
                            v.wait_ge(h_ch, s_chn.val)
                            s_fpz.inc(v.nop())

                        @block.gpsimd
                        def _(g, hg=hg):
                            for htl in range(4):
                                ht = 4 * hg + htl
                                g.wait_ge(h_fe, fe0 + ht + 1)
                                g.dma_start(out=hT_loc[128 * ht:128 * (ht + 1), :],
                                            in_=hTt[ht % 2][:, :]).then_inc(h_fh, 16)
                                s_fh.val += 16


                    @block.vector
                    def _(v):
                        v.wait_ge(h_f2, s_f2.val)
                        s_chn.inc(v.tensor_copy(hsrow[:, :], h2row[:, :]))

                    @block.scalar
                    def _(s):
                        s.wait_ge(h_ch, s_chn.val)
                        s_chn.inc(s.activation(invrow[:, :], hsrow[:, :], AF.Ln,
                                               scale=1.0 / HF, bias=consts[0:1, 1:2]))
                        s.wait_ge(h_ch, s_chn.val)
                        s_chn.inc(s.activation(invrow[:, :], invrow[:, :], AF.Exp, scale=-0.5))

                    @block.tensor
                    def _(t):
                        t.wait_ge(h_ch, s_chn.val)
                        for tt in range(8):
                            ins = t.matmul(pcolF[:, tt:tt + 1],
                                           invrow[0:1, 128 * tt:128 * (tt + 1)],
                                           ones32[0:1, 0:1], start=True, stop=True)
                        s_fp.inc(ins)

                    @block.vector
                    def _(v):
                        v.wait_ge(h_fp, s_fp.val)
                        s_fe.inc(v.tensor_copy(inv_col[:, :], pcolF[:, :]))

                esP = ExitStack()
                with esP:
                    wpqH = [esP.enter_context(nc.sbuf_tensor(f"wpqH{i}", [128, 32, 512], BF16))
                            for i in range(3)]
                    hTl = [esP.enter_context(nc.sbuf_tensor(f"hTl{i}", [128, 64, 128], BF16))
                           for i in range(2)]
                    pp = esP.enter_context(nc.psum_tensor("pp", [128, 1024], F32))
                    fpP0 = s_fp.val
                    fhP0 = s_fh.val
                    fwP0 = s_fw.val

                    for k in range(8):
                        dq, hh = k // 2, k % 2

                        @block.sync
                        def _(sy, k=k, dq=dq, hh=hh):
                            if k == 0:
                                sy.wait_ge(h_fe, s_fe.val)
                            if k >= 3:
                                sy.wait_ge(h_fp, fpP0 + 8 * ((k - 3) // 2 + 1))
                            for htl in range(32):
                                ht = 32 * hh + htl
                                sy.dma_start_transpose(
                                    wpqH[k % 3][:, htl, :],
                                    wqp_blk[ht, 512 * dq:512 * (dq + 1), :]
                                ).then_inc(h_fw, 16)
                            s_fw.val += 32 * 16

                    for dq in range(4):
                        for tt in range(8):
                            idx = 8 * dq + tt

                            @block.gpsimd
                            def _(g, dq=dq, tt=tt, idx=idx):
                                if idx == 0:
                                    g.wait_ge(h_fe, s_fe.val)
                                if idx >= 2:
                                    g.wait_ge(h_fp, fpP0 + idx - 1)
                                g.dma_start(
                                    out=hTl[idx % 2][:, :, :],
                                    in_=hT_loc.ap()[:, 128 * tt:128 * (tt + 1)].rearrange(
                                        "(ht p) t -> p ht t", p=128)).then_inc(h_fh, 16)
                                s_fh.val += 16

                            @block.tensor
                            def _(t, dq=dq, tt=tt, idx=idx):
                                t.wait_ge(h_fh, fhP0 + 16 * (idx + 1))
                                t.wait_ge(h_fw, fwP0 + 512 * (2 * dq + 2))
                                par = idx % 2
                                if idx >= 2:
                                    t.wait_ge(h_fs, s_fs.val - 1)
                                for ht in range(64):
                                    ins = t.matmul(pp[:, 512 * par:512 * (par + 1)],
                                                   hTl[idx % 2][:, ht, :],
                                                   wpqH[(2 * dq + ht // 32) % 3][:, ht % 32, :],
                                                   start=(ht == 0), stop=(ht == 63))
                                s_fp.inc(ins)

                        @block.scalar
                        def _(s, dq=dq):
                            base0 = s_fp.val - 8
                            for tt in range(8):
                                s.wait_ge(h_fp, base0 + tt + 1)
                                par = (8 * dq + tt) % 2
                                ins = s.activation(g_sb[:, tt, 512 * dq:512 * (dq + 1)],
                                                   pp[:, 512 * par:512 * (par + 1)],
                                                   AF.Silu, scale=inv_col[:, tt:tt + 1])
                                s_fs.inc(ins)

        if STAGE >= 6:
            h_ow = es.enter_context(nc.semaphore("s_ow"))
            h_op = es.enter_context(nc.semaphore("s_op"))
            h_oe = es.enter_context(nc.semaphore("s_oe"))
            h_oc = es.enter_context(nc.semaphore("s_oc"))
            h_ox = es.enter_context(nc.semaphore("s_ox"))
            s_ow, s_op, s_oe = Sem(h_ow), Sem(h_op), Sem(h_oe)
            s_oc, s_ox = Sem(h_oc), Sem(h_ox)

            esO = ExitStack()
            with esO:
                sbO = lambda *a, **k: esO.enter_context(nc.sbuf_tensor(*a, **k))
                wo16 = sbO("wo16", [128, DT, D], BF16)
                csum16 = sbO("csum16", [1, NH, TOK], BF16)
                xow = sbO("xow", [128, D], F32)
                Pb = sbO("Pb", [128, 512], F32)
                obuf = sbO("obuf", [128, D], F32)
                pbc = esO.enter_context(nc.psum_tensor("pbc", [128, TOK], F32))
                po = esO.enter_context(nc.psum_tensor("po", [128, 1024], F32))

                @block.gpsimd
                def _(g):
                    if STAGE >= 5:
                        g.wait_ge(h_fs, s_fs.val)
                    s_ow.inc(g.dma_start(
                        out=wo16[:, :, :],
                        in_=w_o.rearrange("(dt p) c -> p dt c", p=128)), 16)
                    with g.register("quadO") as qd:
                        g.reg_load(qd, sids[0:1, 2:3])
                        for q in range(4):
                            with g.If_eq(qd, q):
                                g.dma_start(
                                    out=csum16[0:1, :, :],
                                    in_=colsum_sh.ap()[:, 1024 * q:1024 * (q + 1)].unsqueeze(0)
                                ).then_inc(h_ow, 16)
                    s_ow.val += 16

                for hb in range(NH):
                    @block.tensor
                    def _(t, hb=hb):
                        t.wait_ge(h_ow, s_ow.val)
                        if hb > 0:
                            t.wait_ge(h_oe, s_oe.val)
                        for ch in range(2):
                            ins = t.matmul(pbc[:, 512 * ch:512 * (ch + 1)],
                                           ones16[:, :],
                                           csum16[0:1, hb, 512 * ch:512 * (ch + 1)],
                                           start=True, stop=True)
                        s_op.inc(ins)

                    @block.vector
                    def _(v, hb=hb):
                        v.wait_ge(h_op, s_op.val)
                        ins = v.tensor_tensor(VT[:, hb, :], VT[:, hb, :],
                                              pbc[:, :], ALU.mult)
                        s_oe.inc(ins)

                for tt in range(8):
                    @block.sync
                    def _(sy, tt=tt):
                        if tt == 0 and STAGE >= 5:
                            sy.wait_ge(h_fs, s_fs.val)
                        if tt > 0:
                            sy.wait_ge(h_oc, s_oc.val)
                        s_ox.inc(sy.dma_start(out=xow[:, :],
                                              in_=x_own[128 * tt:128 * (tt + 1), :]), 16)

                    for dc in range(4):
                        idx = 4 * tt + dc

                        @block.tensor
                        def _(t, tt=tt, dc=dc, idx=idx):
                            if idx == 0:
                                t.wait_ge(h_oe, s_oe.val)
                            if idx >= 2:
                                t.wait_ge(h_oc, idx - 1)
                            par = idx % 2
                            for hb in range(NH):
                                ins = t.matmul(po[:, 512 * par:512 * (par + 1)],
                                               VT[:, hb, 128 * tt:128 * (tt + 1)],
                                               wo16[:, hb, 512 * dc:512 * (dc + 1)],
                                               start=(hb == 0), stop=(hb == NH - 1))
                            s_op.inc(ins)

                        @block.vector
                        def _(v, tt=tt, dc=dc, idx=idx):
                            v.wait_ge(h_op, s_op.val)
                            v.wait_ge(h_ox, s_ox.val)
                            par = idx % 2
                            s_chn.inc(v.tensor_tensor(Pb[:, :], po[:, 512 * par:512 * (par + 1)],
                                                      g_sb[:, tt, 512 * dc:512 * (dc + 1)],
                                                      ALU.mult))
                            v.wait_ge(h_ch, s_chn.val)
                            ins = v.tensor_tensor(obuf[:, 512 * dc:512 * (dc + 1)],
                                                  Pb[:, :], xow[:, 512 * dc:512 * (dc + 1)],
                                                  ALU.add)
                            s_oc.inc(ins)

                    @block.sync
                    def _(sy, tt=tt):
                        sy.wait_ge(h_oc, s_oc.val)
                        s_ox.inc(sy.dma_start(out=out[128 * tt:128 * (tt + 1), :],
                                              in_=obuf[:, :]), 16)

                @block.sync
                def _(sy):
                    sy.wait_ge(h_ox, s_ox.val)

        if STAGE == 1:
            @block.gpsimd
            def _(g):
                g.wait_ge(h_sync, s_sync.val)
                s_st.inc(g.dma_start(out=dbg, in_=xnT_sh.ap()[0:TOK, 0:D]), 16)
        elif STAGE == 5:
            @block.gpsimd
            def _(g):
                g.wait_ge(h_fs, s_fs.val)
                s_st.inc(g.dma_start(
                    out=dbg.rearrange("(tt p) d -> p tt d", p=128),
                    in_=g_sb[:, :, :]), 16)
        elif STAGE == 4:
            @block.gpsimd
            def _(g):
                g.wait_ge(h_sync, s_sync.val)
                s_st.inc(g.dma_start(out=dbg[0:1024, 0:128], in_=wqgu_blk[0, 0:1024, :]), 16)
                s_st.inc(g.dma_start(out=dbg[0:1024, 128:256], in_=wqgu_blk[7, 9000:10024, :]), 16)
                s_st.inc(g.dma_start(out=dbg[0:1024, 256:384], in_=wqp_blk[0, 0:1024, :]), 16)
                s_st.inc(g.dma_start(out=dbg[0:1024, 384:512], in_=wqp_blk[33, 1024:2048, :]), 16)
        elif STAGE == 3:
            @block.gpsimd
            def _(g):
                g.wait_ge(h_sync, s_sync.val)
                s_st.inc(g.dma_start(
                    out=dbg[0:32, 0:2048],
                    in_=colsum_sh.ap().rearrange("h (a t) -> (h a) t", a=2)), 16)
        elif STAGE == 35:
            @block.gpsimd
            def _(g):
                g.wait_ge(h_cst, s_cst.val)
                s_st.inc(g.dma_start(out=dbg[0:128, 0:32], in_=racc[:, :]), 16)
                s_st.inc(g.dma_start(out=dbg[0:128, 32:40], in_=rsum[:, :]), 16)
                s_st.inc(g.dma_start(out=dbg[0:128, 40:48], in_=arec[:, :]), 16)
                s_st.inc(g.dma_start(out=dbg[0:128, 48:56], in_=a16[:, :]), 16)
                s_st.inc(g.dma_start(out=dbg[128:256, 0:2048], in_=Egrp[:, 0, 0:2048]), 16)
                s_st.inc(g.dma_start(out=dbg[256:384, 0:2048], in_=Egrp[:, 0, 2048:4096]), 16)
                s_st.inc(g.dma_start(out=dbg[384:386, 0:2048],
                                     in_=colsum_sh.ap()[0:1, :].rearrange("h (a t) -> (h a) t", a=2)), 16)
                s_st.inc(g.dma_start(out=dbg[512:640, 0:2048], in_=QT[:, 0:2048]), 16)
                s_st.inc(g.dma_start(out=dbg[640:768, 0:2048], in_=KT[:, 0:2048]), 16)
                s_st.inc(g.dma_start(out=dbg[386:387, 0:2048],
                                     in_=cget.ap()[0:1, 0:4, :].rearrange("a g t -> a (g t)")), 16)
                s_st.inc(g.dma_start(out=dbg[387:388, 0:2048],
                                     in_=cget.ap()[0:1, 4:8, :].rearrange("a g t -> a (g t)")), 16)
                s_st.inc(g.dma_start(out=dbg[768:896, 0:32],
                                     in_=aget.ap().rearrange("p g j -> p (g j)")), 16)
                s_st.inc(g.dma_start(out=dbg[896:1024, 0:2048], in_=eget[:, :]), 16)
                s_st.inc(g.dma_start(out=dbg[768:896, 32:40], in_=rget[:, :]), 16)
                s_st.inc(g.dma_start(out=dbg[768:896, 40:72], in_=r2get[:, :]), 16)
        elif STAGE == 2:
            @block.gpsimd
            def _(g):
                g.wait_ge(h_qe, s_qe.val)
                s_st.inc(g.dma_start(out=dbg[0:128, 0:2048], in_=QT[:, 0, 0:2048]), 16)
                s_st.inc(g.dma_start(out=dbg[128:256, 0:2048], in_=QT[:, 0, 2048:4096]), 16)
                s_st.inc(g.dma_start(out=dbg[256:384, 0:2048], in_=KT[:, 0, 0:2048]), 16)
                s_st.inc(g.dma_start(out=dbg[384:512, 0:2048], in_=KT[:, 0, 2048:4096]), 16)
                s_st.inc(g.dma_start(out=dbg[512:640, 0:1024], in_=VT[:, 0, :]), 16)
                s_st.inc(g.dma_start(out=dbg[640:768, 0:1024], in_=VT[:, 9, :]), 16)

            @block.gpsimd
            def _(g):
                g.wait_ge(h_st, s_st.val)

    return nc


_NC_CACHE = None


def _in_maps(X, Wqkv, Wo, Wgu, Wp):
    in_maps, meta = [], []
    for c in range(8):
        m = c % 2
        pair = c // 2
        b = pair % 2
        quad = {0: 0, 1: 1, 4: 2, 5: 3, 2: 0, 3: 1, 6: 2, 7: 3}[c]
        own = 1024 * quad
        hs = 128 * HPM * m
        w_qk_c = np.concatenate(
            [Wqkv[:, hs:hs + 1024], Wqkv[:, 2048 + hs:2048 + hs + 1024]], axis=1)
        idv = np.zeros((1, 16), np.int32)
        idv[0, 0] = c
        idv[0, 1] = m
        idv[0, 2] = quad
        in_maps.append({
            "x_half": np.ascontiguousarray(X[b, HALF * m:HALF * (m + 1), :]),
            "x_own": np.ascontiguousarray(X[b, own:own + TOK, :]),
            "w_qk": np.ascontiguousarray(w_qk_c),
            "w_v": np.ascontiguousarray(Wqkv[:, 4096:6144]),
            "w_o": Wo,
            "wgu_half": np.ascontiguousarray(Wgu[HF * m:HF * (m + 1), :]),
            "wp_half": np.ascontiguousarray(Wp[1024 * m:1024 * (m + 1), :]),
            "ids": idv,
        })
        meta.append((b, own))
    return in_maps, meta


def kernel(L0_attn_x, L0_attn_w_qkv, L0_attn_w_o, L0_ffn_w_gate_up, L0_ffn_w_proj):
    global _NC_CACHE
    X = np.ascontiguousarray(np.asarray(L0_attn_x, dtype=np.float32))
    Wqkv = np.ascontiguousarray(np.asarray(L0_attn_w_qkv, dtype=np.float32))
    Wo = np.ascontiguousarray(np.asarray(L0_attn_w_o, dtype=np.float32))
    Wgu = np.ascontiguousarray(np.asarray(L0_ffn_w_gate_up, dtype=np.float32))
    Wp = np.ascontiguousarray(np.asarray(L0_ffn_w_proj, dtype=np.float32))

    if _NC_CACHE is None:
        _NC_CACHE = build_program()
    nc = _NC_CACHE

    in_maps, meta = _in_maps(X, Wqkv, Wo, Wgu, Wp)
    res = run_bass_kernel_spmd(nc, in_maps, core_ids=list(range(8)))
    outp = np.zeros((B, S, D), np.float32)
    for c in range(8):
        b, own = meta[c]
        outp[b, own:own + TOK, :] = res.results[c]["out"]
    if STAGE < 99:
        kernel.debug = [r["dbg"] for r in res.results]
    return outp
